# version 04 (final)
# baseline (speedup 1.0000x reference)
import numpy as np
import ml_dtypes
from contextlib import ExitStack

import concourse.bass as bass
import concourse.mybir as mybir
from concourse.bass_utils import run_bass_kernel_spmd

F32 = mybir.dt.float32
BF16 = mybir.dt.bfloat16
AF = mybir.ActivationFunctionType
OP = mybir.AluOpType

DM = 1024
DI = 2048
HF = 4096
DST = 8
L = 4096
T = 256
NCH = L // T
NDM = 8
NDT = 16
NCT = 64
FG = 4
NGR = NDT // FG
XPW = T + 3

G_N = 0
DBG_ZERO_HALO = False
MID_DUMPS = False
DBG_SERIAL_CHUNKS = False
TRUNC_STAGE = None
PAD_TE = 0
PAD_MM = False
PAD_MM_M = 128
PAD_MM_GROUP = False
PAD_MM_FORCE_BANKS = None


class _StopEmit(Exception):
    pass

ENGS = ("sy", "te", "sc", "ve", "gp")


class Sem:
    __slots__ = ("name", "n", "h")

    def __init__(self, name):
        self.name, self.n, self.h = name, 0, None


class Action:
    __slots__ = ("waits", "incs", "fn", "tag")

    def __init__(self, waits, incs, fn, tag):
        self.waits, self.incs, self.fn, self.tag = waits, incs, fn, tag


class Builder:
    def __init__(self):
        self.prog = {e: [] for e in ENGS}
        self.sems = {}

    def sem(self, name):
        s = Sem(name)
        self.sems[name] = s
        return s

    def act(self, eng, fn, waits=(), incs=(), tag=""):
        waits = [(s, v) for (s, v) in waits if v > 0]
        incs = list(incs)
        for (s, d) in incs:
            s.n += d
        self.prog[eng].append(Action(waits, incs, fn, tag))

    def simulate(self):
        cnt = {s: 0 for s in self.sems.values()}
        ptr = {e: 0 for e in ENGS}
        total = sum(len(v) for v in self.prog.values())
        done = 0
        while done < total:
            progress = False
            for e in ENGS:
                while ptr[e] < len(self.prog[e]):
                    a = self.prog[e][ptr[e]]
                    if all(cnt[s] >= v for (s, v) in a.waits):
                        for (s, d) in a.incs:
                            cnt[s] += d
                        ptr[e] += 1
                        done += 1
                        progress = True
                    else:
                        break
            if not progress:
                msg = []
                for e in ENGS:
                    if ptr[e] < len(self.prog[e]):
                        a = self.prog[e][ptr[e]]
                        blocked = [(s.name, v, cnt[s]) for (s, v) in a.waits if cnt[s] < v]
                        msg.append(f"{e}[{ptr[e]}] tag={a.tag} blocked on {blocked}")
                raise RuntimeError("DEADLOCK in schedule:\n" + "\n".join(msg))
        for e in ENGS:
            for a in self.prog[e]:
                for (s, v) in a.waits:
                    assert v <= s.n, f"wait {s.name}>={v} exceeds final {s.n} ({a.tag})"


class Banks:
    def __init__(self, tensors):
        self.slots = [{"t": t, "free": []} for t in tensors]
        self.i = 0

    def acquire(self):
        s = self.slots[self.i % 8]
        self.i += 1
        w = s["free"]
        s["free"] = []
        return s, w

    @staticmethod
    def release(slot, sem, val):
        slot["free"] = [(sem, val)]


class Rot:
    def __init__(self, aps):
        self.aps = aps
        self.free = [[] for _ in aps]
        self.i = 0

    def next(self):
        k = self.i % len(self.aps)
        self.i += 1
        w = self.free[k]
        self.free[k] = []
        return k, self.aps[k], w

    def consumed(self, k, sem, val):
        self.free[k].append((sem, val))


def ap3(t, off, dims):
    return bass.AP(t, off, dims)


def build_nc(nch=NCH, dumps=()):
    nc = bass.Bass("TRN2", target_bir_lowering=False)
    B = Builder()

    xT = nc.dram_tensor("xT", [DM, L], F32, kind="ExternalInput")
    outT = nc.dram_tensor("outT", [DM, L], F32, kind="ExternalOutput")
    wcat_f = nc.dram_tensor("wcat_f", [NCT * 128, 8 * 128], F32, kind="ExternalInput")
    w2_f = nc.dram_tensor("w2_f", [HF, DM], F32, kind="ExternalInput")
    wout_f = nc.dram_tensor("wout_f", [8 * 128, 16 * 128], F32, kind="ExternalInput")
    wxp_f = nc.dram_tensor("wxp_f", [128, 16 * 128], F32, kind="ExternalInput")
    wdt_f = nc.dram_tensor("wdt_f", [64, DI], F32, kind="ExternalInput")
    alog_f = nc.dram_tensor("alog_f", [DI, DST], F32, kind="ExternalInput")
    ut_f = nc.dram_tensor("ut_f", [8, DI], F32, kind="ExternalInput")
    v_f = nc.dram_tensor("v_f", [8, 8], F32, kind="ExternalInput")
    lrs_f = nc.dram_tensor("lrs_f", [1, 1], F32, kind="ExternalInput")
    convw_f = nc.dram_tensor("convw_f", [DI, 4], F32, kind="ExternalInput")
    convb_f = nc.dram_tensor("convb_f", [16, 128], F32, kind="ExternalInput")
    dtb_f = nc.dram_tensor("dtb_f", [16, 128], F32, kind="ExternalInput")
    dvec_f = nc.dram_tensor("dvec_f", [16, 128], F32, kind="ExternalInput")
    b1_f = nc.dram_tensor("b1_f", [32, 128], F32, kind="ExternalInput")
    b2_f = nc.dram_tensor("b2_f", [8, 128], F32, kind="ExternalInput")
    ident_f = nc.dram_tensor("ident_f", [128, 128], F32, kind="ExternalInput")
    ones1_f = nc.dram_tensor("ones1_f", [1, 128], F32, kind="ExternalInput")
    onegs_f = nc.dram_tensor("onegs_f", [1, 128], F32, kind="ExternalInput")
    c1024_f = nc.dram_tensor("c1024_f", [128, 128], BF16, kind="ExternalInput")
    sel_f = nc.dram_tensor("sel_f", [16, 16 * 128], F32, kind="ExternalInput")
    wcat_st = nc.dram_tensor("wcat_st", [NCT * 128, 8 * 128], BF16, kind="Internal")
    w2_st = nc.dram_tensor("w2_st", [HF, DM], BF16, kind="Internal")
    wout_st = nc.dram_tensor("wout_st", [8 * 128, 16 * 128], BF16, kind="Internal")

    ctx = ExitStack()
    sb = lambda n, s, d=F32: ctx.enter_context(nc.sbuf_tensor(n, s, d))
    ps = lambda n, s: ctx.enter_context(nc.psum_tensor(n, s, F32))

    a1 = sb("a1", [128, 4096])
    a2 = sb("a2", [128, NDT * XPW])
    a3 = sb("a3", [128, 4096])
    a4 = sb("a4", [128, 4096], BF16)
    a5 = sb("a5", [128, 4096])
    a6 = sb("a6", [128, 4096])
    a7 = sb("a7", [128, 4096])
    a8 = sb("a8", [128, 2048])
    xn_s = sb("xn_s", [128, 2048], BF16)
    z_s = sb("z_s", [128, 4096])
    oacc_s = sb("oacc_s", [128, 2048])
    hpool_s = sb("hpool_s", [128, 2 * FG * T])
    hcar_s = sb("hcar_s", [128, 128])
    brep_s = sb("brep_s", [128, 2 * T])
    crep_s = sb("crep_s", [128, 2 * T])
    bcst_s = sb("bcst_s", [128, T])
    rows_s = sb("rows_s", [1, 1280])
    lnt_s = sb("lnt_s", [128, 3 * T])
    tmpln_s = sb("tmpln_s", [128, 2 * T])
    cacc_s = sb("cacc_s", [128, 2 * T])
    upre_s = sb("upre_s", [128, 2 * T])
    dtlr_s = sb("dtlr_s", [64, T], BF16)
    halo_s = sb("halo_s", [128, 48])
    sel_s = sb("sel_s", [128, 16 * 128])
    id_s = sb("id_s", [128, 128])
    ones1_s = sb("ones1_s", [1, 128])
    onegs_s = sb("onegs_s", [1, 128])
    c1024_s = sb("c1024_s", [128, 128], BF16)
    alog_s = sb("alog_s", [128, 128])
    uv_s = sb("uv_s", [128, 128])
    ea_s = sb("ea_s", [128, 128])
    nega_s = sb("nega_s", [128, 128])
    snegc_s = sb("snegc_s", [128, 1])
    cwc_s = sb("cwc_s", [128, 64])
    cbc_s = sb("cbc_s", [128, 16])
    ndtb_s = sb("ndtb_s", [128, 16])
    dcol_s = sb("dcol_s", [128, 16])
    b1c_s = sb("b1c_s", [128, 32])
    b2c_s = sb("b2c_s", [128, 8])
    wxp_s = sb("wxp_s", [128, 16 * 128], BF16)
    wdt_s = sb("wdt_s", [64, DI], BF16)
    wcat_b = sb("wcat_b", [128, 2 * 1024], BF16)
    w2_b = sb("w2_b", [128, 2 * 1024], BF16)
    wout_b = sb("wout_b", [128, 2 * 2048], BF16)

    banks = Banks([ps(f"pb{i}", [128, T]) for i in range(8)])

    a1_bf = a1.ap().bitcast(BF16)
    a8_bf = a8.ap().bitcast(BF16)

    def h1_ap(k):
        return a1_bf[:, k * T:(k + 1) * T]

    dneg, xpart, yv, sig = a1, a2, a2, a3
    ubf = ygbf = a4
    u_s = a5
    x_s, yg = a6, a6
    du, out_s = a7, a7
    xbf_ap = a8_bf[:, 0:2048]
    x2bf_ap = a8_bf[:, 2048:4096]

    def slot3(arena, off, row):
        return ap3(arena, off, [[row, 128], [T, FG], [1, T]])

    dbu_off_v = [0, FG * T]
    dbu_off_g = [2 * FG * T, 3 * FG * T]
    p_off_v = [0, FG * T]
    p_off_g = [2 * FG * T, 3 * FG * T]
    dA_off = [0, FG * T]

    sQP = B.sem("sQP")
    sQW = B.sem("sQW")
    sQX = B.sem("sQX")
    sQO = B.sem("sQO")
    sT = B.sem("sT")
    sS = B.sem("sS")
    sV = B.sem("sV")
    sG = B.sem("sG")

    marks = {}

    pre = [
        (id_s[:], ident_f[:]), (ones1_s[:], ones1_f[:]), (onegs_s[:], onegs_f[:]),
        (c1024_s[:], c1024_f[:]), (sel_s[64:80, :], sel_f[:]),
        (wxp_s[:], wxp_f[:]), (wdt_s[:], wdt_f[:]),
        (a6[0:8, 0:2048], ut_f[:]),
        (a6[0:8, 2048:2056], v_f[:]),
        (a6[0:1, 2056:2057], lrs_f[:]),
        (a7[0:16, 0:128], convb_f[:]), (a7[0:16, 128:256], dtb_f[:]),
        (a7[0:16, 256:384], dvec_f[:]), (a7[0:32, 384:512], b1_f[:]),
        (a7[0:8, 512:640], b2_f[:]),
    ]
    for (dst, src) in pre:
        B.act("gp", (lambda d, s: lambda e: e.dma_start(out=d, in_=s))(dst, src),
              incs=[(sQP, 16)], tag="pre_dma")
    for dt in range(NDT):
        B.act("gp", (lambda dt: lambda e: e.dma_start(
            out=alog_s[:, dt * 8:(dt + 1) * 8],
            in_=alog_f[dt * 128:(dt + 1) * 128, :]))(dt),
            incs=[(sQP, 16)], tag="pre_alog")
        B.act("gp", (lambda dt: lambda e: e.dma_start(
            out=cwc_s[:, dt * 4:(dt + 1) * 4],
            in_=convw_f[dt * 128:(dt + 1) * 128, :]))(dt),
            incs=[(sQP, 16)], tag="pre_cw")
    n_const = sQP.n
    for i in range(4):
        B.act("gp", (lambda i: lambda e: e.dma_start(
            out=wcat_st[i * 2048:(i + 1) * 2048, :],
            in_=wcat_f[i * 2048:(i + 1) * 2048, :]))(i),
            incs=[(sQP, 16)], tag="stage_wcat")
    for i in range(2):
        B.act("gp", (lambda i: lambda e: e.dma_start(
            out=w2_st[i * 2048:(i + 1) * 2048, :],
            in_=w2_f[i * 2048:(i + 1) * 2048, :]))(i),
            incs=[(sQP, 16)], tag="stage_w2")
    B.act("gp", lambda e: e.dma_start(out=wout_st[:], in_=wout_f[:]),
          incs=[(sQP, 16)], tag="stage_wout")
    n_pro = sQP.n

    tr_jobs = [
        (a7[0:16, 0:128], 16, cbc_s[:, 0:16], False),
        (a7[0:16, 128:256], 16, ndtb_s[:, 0:16], True),
        (a7[0:16, 256:384], 16, dcol_s[:, 0:16], False),
        (a7[0:32, 384:512], 32, b1c_s[:, 0:32], False),
        (a7[0:8, 512:640], 8, b2c_s[:, 0:8], False),
    ]
    for idx, (src, r, dst, neg) in enumerate(tr_jobs):
        slot, bw = banks.acquire()
        bank = slot["t"]
        B.act("te", (lambda s_, r_, b_: lambda e: e.transpose(
            b_[:, 0:r_], s_, id_s[0:r_, 0:r_]))(src, r, bank),
            waits=bw + [(sQP, n_const)], incs=[(sT, 1)], tag=f"pre_tr{idx}")
        tv = sT.n
        B.act("sc", (lambda b_, r_, d_, n_: lambda e: e.activation(
            d_, b_[:, 0:r_], AF.Copy, scale=(-1.0 if n_ else 1.0)))(bank, r, dst, neg),
            waits=[(sT, tv)], incs=[(sS, 1)], tag=f"pre_trev{idx}")
        Banks.release(slot, sS, sS.n)
    for half in range(2):
        slot, bw = banks.acquire()
        for j in range(8):
            dt = half * 8 + j
            B.act("te", (lambda dt, b_, j: lambda e: e.matmul(
                b_[:, j * 8:(j + 1) * 8], a6[0:8, dt * 128:(dt + 1) * 128],
                a6[0:8, 2048:2056], start=True, stop=True))(dt, slot["t"], j),
                waits=(bw + [(sQP, n_const)]) if j == 0 else [],
                incs=[(sT, 1)], tag=f"pre_uv{dt}")
        tv = sT.n
        B.act("sc", (lambda b_, half: lambda e: e.copy(
            uv_s[:, half * 64:(half + 1) * 64], b_[:, 0:64]))(slot["t"], half),
            waits=[(sT, tv)], incs=[(sS, 1)], tag=f"pre_uvev{half}")
        Banks.release(slot, sS, sS.n)
    slot, bw = banks.acquire()
    B.act("te", (lambda b_: lambda e: e.matmul(
        b_[:, 0:1], onegs_s[:], a6[0:1, 2056:2057], start=True, stop=True))(slot["t"]),
        waits=bw + [(sQP, n_const)], incs=[(sT, 1)], tag="pre_sneg")
    B.act("sc", (lambda b_: lambda e: e.copy(snegc_s[:], b_[:, 0:1]))(slot["t"]),
          waits=[(sT, sT.n)], incs=[(sS, 1)], tag="pre_snegev")
    Banks.release(slot, sS, sS.n)
    B.act("sc", lambda e: e.activation(ea_s[:], alog_s[:], AF.Exp),
          waits=[(sQP, n_const)], incs=[(sS, 1)], tag="pre_ea")
    sc_pre = sS.n
    B.act("ve", lambda e: e.scalar_tensor_tensor(
        nega_s[:], uv_s[:], snegc_s[:], ea_s[:], OP.mult, OP.add),
        waits=[(sS, sc_pre)], incs=[(sV, 1)], tag="pre_negA")
    B.act("ve", lambda e: e.memset(hcar_s[:], 0.0), incs=[(sV, 1)], tag="pre_hc0")
    for dt in range(NDT):
        B.act("ve", (lambda dt: lambda e: e.memset(
            ap3(xpart, dt * XPW, [[NDT * XPW, 128], [1, 3]]), 0.0))(dt),
            incs=[(sV, 1)], tag=f"pre_halo0_{dt}")
    marks["pre_te"] = sT.n

    rot_cat = Rot([wcat_b[:, 0:1024], wcat_b[:, 1024:2048]])
    rot_w2 = Rot([w2_b[:, 0:1024], w2_b[:, 1024:2048]])
    rot_wo = Rot([wout_b[:, 0:2048], wout_b[:, 2048:4096]])
    rot_brep = Rot([brep_s[:, 0:T], brep_s[:, T:2 * T]])
    rot_crep = Rot([crep_s[:, 0:T], crep_s[:, T:2 * T]])
    rot_dA = Rot(dA_off)
    rot_hp = Rot([0, FG * T])
    rot_dbu = {False: Rot(dbu_off_v), True: Rot(dbu_off_g)}
    rot_p = {False: Rot(p_off_v), True: Rot(p_off_g)}
    rot_cacc = Rot([cacc_s[:, 0:T], cacc_s[:, T:2 * T]])
    rot_upre = Rot([upre_s[:, 0:T], upre_s[:, T:2 * T]])
    rot_tmp = Rot([tmpln_s[:, 0:T], tmpln_s[:, T:2 * T]])

    def _tr(k, c, nch):
        if c == nch - 1 and TRUNC_STAGE is not None and TRUNC_STAGE <= k:
            raise _StopEmit()

    for c in range(nch):
      try:
        t0 = c * T
        xw = [(sQP, n_pro), (sT, marks["pre_te"])] if c == 0 else \
             [(sV, marks[("xfree", c - 1)])]
        if c > 0 and DBG_SERIAL_CHUNKS:
            xw.append((sV, marks[("attnout", c - 1)]))
            xw.append((sQO, marks[("ostore", c - 1)]))
        if c > 0 and G_N > 0:
            xw.append((sV, marks[("ysum", c - 1)]))
        B.act("sy", (lambda t0: lambda e: e.dma_start(
            out=ap3(x_s, 0, [[4096, 128], [T, NDM], [1, T]]),
            in_=ap3(xT, t0, [[L, 128], [128 * L, NDM], [1, T]])))(t0),
            waits=xw, incs=[(sQX, 16)], tag=f"xload{c}")
        xld = sQX.n
        _tr(1, c, nch)

        def slab_load(rot, st_tensor, j, cols, tag):
            k, ap, bw = rot.next()
            w = list(bw)
            if c > 0 and DBG_SERIAL_CHUNKS:
                w.append((sV, marks[("attnout", c - 1)]))
            B.act("sy", (lambda ap, st_tensor, j, cols: lambda e: e.dma_start(
                out=ap, in_=st_tensor[j * 128:(j + 1) * 128, 0:cols]))(ap, st_tensor, j, cols),
                waits=w, incs=[(sQW, 16)], tag=tag)
            return k, ap, sQW.n

        xbw = [(sQX, xld)]
        if c > 0:
            xbw.append((sV, marks[("scan_done", c - 1)]))
        B.act("sc", lambda e: e.copy(xbf_ap, x_s[:, 0:2048]),
              waits=xbw, incs=[(sS, 1)], tag=f"xbf{c}")
        B.act("sc", lambda e: e.activation(x2bf_ap, x_s[:, 0:2048], AF.Square),
              incs=[(sS, 1)], tag=f"x2bf{c}")
        sc_stats = sS.n
        slotM, bwM = banks.acquire()
        slotE, bwE = banks.acquire()
        for i in range(NDM):
            B.act("te", (lambda i, b_: lambda e: e.matmul(
                b_[:], c1024_s[:], xbf_ap[:, i * T:(i + 1) * T],
                start=(i == 0), stop=(i == NDM - 1)))(i, slotM["t"]),
                waits=(bwM + [(sS, sc_stats)]) if i == 0 else [], incs=[(sT, 1)],
                tag=f"mean{c}")
        for i in range(NDM):
            B.act("te", (lambda i, b_: lambda e: e.matmul(
                b_[:], c1024_s[:], x2bf_ap[:, i * T:(i + 1) * T],
                start=(i == 0), stop=(i == NDM - 1)))(i, slotE["t"]),
                waits=bwE if i == 0 else [], incs=[(sT, 1)], tag=f"ex2{c}")
        te_stats = sT.n
        marks[("stats_mm", c)] = te_stats
        lnt0 = lnt_s[:, 0:T]
        lnt1 = lnt_s[:, T:2 * T]
        lnt2 = lnt_s[:, 2 * T:3 * T]
        lw = [(sT, te_stats)]
        if c > 0:
            lw.append((sV, marks[("lnt_free", c - 1)]))
        B.act("sc", (lambda b_: lambda e: e.copy(lnt0, b_[:]))(slotM["t"]),
              waits=lw, incs=[(sS, 1)], tag=f"meanev{c}")
        B.act("sc", (lambda b_: lambda e: e.copy(lnt1, b_[:]))(slotE["t"]),
              incs=[(sS, 1)], tag=f"ex2ev{c}")
        sc_rows = sS.n
        Banks.release(slotM, sS, sc_rows)
        Banks.release(slotE, sS, sc_rows)
        B.act("ve", lambda e: e.tensor_tensor(lnt2, lnt0, lnt0, OP.mult),
              waits=[(sS, sc_rows)], incs=[(sV, 1)], tag=f"sq{c}")
        B.act("ve", lambda e: e.scalar_tensor_tensor(
            lnt2, lnt1, 1e-5, lnt2, OP.add, OP.subtract),
            incs=[(sV, 1)], tag=f"var{c}")
        ve_var = sV.n
        B.act("sc", lambda e: e.activation(lnt1, lnt2, AF.Sqrt),
              waits=[(sV, ve_var)], incs=[(sS, 1)], tag=f"sqrt{c}")
        sc_sqrt = sS.n
        B.act("ve", lambda e: e.reciprocal(lnt2, lnt1),
              waits=[(sS, sc_sqrt)], incs=[(sV, 1)], tag=f"recip{c}")
        B.act("ve", lambda e: e.tensor_tensor(lnt1, lnt0, lnt2, OP.mult),
              incs=[(sV, 1)], tag=f"mr{c}")
        ve_mr = sV.n
        marks[("rep_mm", c)] = sT.n
        xnw = []
        if c > 0:
            xnw.append((sT, marks[("cat_mm", c - 1)]))
        for i in range(NDM):
            kk, tmp, bwT = rot_tmp.next()
            B.act("ve", (lambda i, tmp: lambda e: e.tensor_tensor(
                tmp, x_s[:, i * T:(i + 1) * T], lnt2, OP.mult))(i, tmp),
                waits=(xnw if i == 0 else []) + bwT, incs=[(sV, 1)], tag=f"xr{c}_{i}")
            B.act("ve", (lambda i, tmp: lambda e: e.tensor_tensor(
                xn_s[:, i * T:(i + 1) * T], tmp, lnt1, OP.subtract))(i, tmp),
                incs=[(sV, 1)], tag=f"xn{c}_{i}")
            rot_tmp.consumed(kk, sV, sV.n)
        ve_xn = sV.n
        marks[("lnt_free", c)] = ve_xn
        _tr(2, c, nch)

        zxw = [(sV, marks[("gate", c - 1)])] if c > 0 else []
        h1w = [(sV, marks[("du", c - 1)])] if c > 0 else []
        for j in range(NCT):
            kk, wap, wq = slab_load(rot_cat, wcat_st, j, 1024, f"wcat{c}_{j}")
            slot, bw = banks.acquire()
            for k in range(8):
                B.act("te", (lambda b_, wap, k: lambda e: e.matmul(
                    b_[:], wap[:, k * 128:(k + 1) * 128], xn_s[:, k * T:(k + 1) * T],
                    start=(k == 0), stop=(k == 7)))(slot["t"], wap, k),
                    waits=(bw + [(sQW, wq)] +
                           ([(sV, ve_xn)] if j == 0 else [])) if k == 0 else [],
                    incs=[(sT, 1)], tag=f"cat{c}_{j}")
            rot_cat.consumed(kk, sT, sT.n)
            tv = sT.n
            if j < 16:
                B.act("sc", (lambda b_, j: lambda e: e.copy(
                    ap3(xpart, j * XPW + 3, [[NDT * XPW, 128], [1, T]]), b_[:]))(slot["t"], j),
                    waits=[(sT, tv)] + (zxw if j == 0 else []), incs=[(sS, 1)],
                    tag=f"xpev{c}_{j}")
            elif j < 32:
                B.act("sc", (lambda b_, j: lambda e: e.activation(
                    z_s[:, (j - 16) * T:(j - 15) * T], b_[:], AF.Silu))(slot["t"], j),
                    waits=[(sT, tv)], incs=[(sS, 1)], tag=f"zev{c}_{j}")
            else:
                B.act("sc", (lambda b_, j: lambda e: e.activation(
                    h1_ap(j - 32), b_[:], AF.Gelu,
                    bias=b1c_s[:, j - 32:j - 31]))(slot["t"], j),
                    waits=[(sT, tv)] + (h1w if j == 32 else []), incs=[(sS, 1)],
                    tag=f"h1ev{c}_{j}")
            Banks.release(slot, sS, sS.n)
            if j >= 32:
                marks[("h1", c, j - 32)] = sS.n
        marks[("cat_mm", c)] = sT.n
        sc_xpart_done = sS.n
        _tr(3, c, nch)
        if c == 0 and MID_DUMPS:
            for nm, t, wv in (("md0_xn", xn_s, [(sV, ve_xn)]),
                              ("md0_xpart", a2, [(sS, sc_xpart_done)]),
                              ("md0_z", z_s, [(sS, sc_xpart_done)]),
                              ("md0_h1", a1, [(sS, sc_xpart_done)]),
                              ("md0_x", a6, [(sQX, xld)])):
                dmp = nc.dram_tensor(nm, list(t.shape), t.dtype, kind="ExternalOutput")
                B.act("sy", (lambda dmp, t: lambda e: e.dma_start(
                    out=dmp[:], in_=t[:]))(dmp, t),
                    waits=wv, incs=[(sQO, 16)], tag=nm)

        fslots = []
        for m in range(8):
            slot, bw = banks.acquire()
            fslots.append((slot, bw))
        for k in range(32):
            kk, wap, wq = slab_load(rot_w2, w2_st, k, 1024, f"w2{c}_{k}")
            for m in range(8):
                slot, bw = fslots[m]
                w = []
                if m == 0:
                    w = [(sQW, wq), (sS, marks[("h1", c, k)])]
                if k == 0:
                    w = w + bw
                B.act("te", (lambda b_, wap, k, m: lambda e: e.matmul(
                    b_[:], wap[:, m * 128:(m + 1) * 128], h1_ap(k),
                    start=(k == 0), stop=(k == 31)))(slot["t"], wap, k, m),
                    waits=w, incs=[(sT, 1)], tag=f"ffn2{c}_{k}_{m}")
            rot_w2.consumed(kk, sT, sT.n)
        te_ffn2 = sT.n
        marks[("ffn2_mm", c)] = te_ffn2
        ow = [(sV, marks[("attnout", c - 1)])] if c > 0 else []
        for m in range(8):
            slot, _ = fslots[m]
            B.act("ve", (lambda b_, m: lambda e: e.scalar_tensor_tensor(
                oacc_s[:, m * T:(m + 1) * T], b_[:], b2c_s[:, m:m + 1],
                x_s[:, m * T:(m + 1) * T], OP.add, OP.add))(slot["t"], m),
                waits=([(sT, te_ffn2), (sQX, xld)] + ow) if m == 0 else [],
                incs=[(sV, 1)], tag=f"oacc{c}_{m}")
            Banks.release(slot, sV, sV.n)
        ve_oacc = sV.n
        marks[("xfree", c)] = ve_oacc
        _tr(4, c, nch)

        if c > 0:
            for dt in range(NDT):
                if DBG_ZERO_HALO:
                    B.act("ve", (lambda dt: lambda e: e.memset(
                        ap3(xpart, dt * XPW, [[NDT * XPW, 128], [1, 3]]), 0.0))(dt),
                        waits=[(sS, sc_xpart_done)] if dt == 0 else [],
                        incs=[(sV, 1)], tag=f"halor{c}_{dt}")
                else:
                    B.act("ve", (lambda dt: lambda e: e.tensor_copy(
                        ap3(xpart, dt * XPW, [[NDT * XPW, 128], [1, 3]]),
                        halo_s[:, dt * 3:dt * 3 + 3]))(dt),
                        waits=[(sS, sc_xpart_done)] if dt == 0 else [],
                        incs=[(sV, 1)], tag=f"halor{c}_{dt}")
        for dt in range(NDT):
            k0, acc0, bw0 = rot_cacc.next()
            B.act("ve", (lambda dt, acc0: lambda e: e.tensor_scalar(
                acc0, ap3(xpart, dt * XPW + 0, [[NDT * XPW, 128], [1, T]]),
                cwc_s[:, dt * 4:dt * 4 + 1], None, OP.mult))(dt, acc0),
                waits=([(sS, sc_xpart_done)] if dt == 0 else []) + bw0,
                incs=[(sV, 1)], tag=f"cv0_{c}_{dt}")
            k1, acc1, bw1 = rot_cacc.next()
            B.act("ve", (lambda dt, acc0, acc1: lambda e: e.scalar_tensor_tensor(
                acc1, ap3(xpart, dt * XPW + 1, [[NDT * XPW, 128], [1, T]]),
                cwc_s[:, dt * 4 + 1:dt * 4 + 2], acc0, OP.mult, OP.add))(dt, acc0, acc1),
                waits=bw1, incs=[(sV, 1)], tag=f"cv1_{c}_{dt}")
            rot_cacc.consumed(k0, sV, sV.n)
            k2, acc2, bw2 = rot_cacc.next()
            B.act("ve", (lambda dt, acc1, acc2: lambda e: e.scalar_tensor_tensor(
                acc2, ap3(xpart, dt * XPW + 2, [[NDT * XPW, 128], [1, T]]),
                cwc_s[:, dt * 4 + 2:dt * 4 + 3], acc1, OP.mult, OP.add))(dt, acc1, acc2),
                waits=bw2, incs=[(sV, 1)], tag=f"cv2_{c}_{dt}")
            rot_cacc.consumed(k1, sV, sV.n)
            ku, upre, bwu = rot_upre.next()
            B.act("ve", (lambda dt, acc2, upre: lambda e: e.scalar_tensor_tensor(
                upre, ap3(xpart, dt * XPW + 3, [[NDT * XPW, 128], [1, T]]),
                cwc_s[:, dt * 4 + 3:dt * 4 + 4], acc2, OP.mult, OP.add))(dt, acc2, upre),
                waits=bwu, incs=[(sV, 1)], tag=f"cv3_{c}_{dt}")
            rot_cacc.consumed(k2, sV, sV.n)
            ve_upre = sV.n
            uw = [(sV, ve_upre)]
            if dt == 0 and c > 0:
                uw.append((sV, marks[("ylast_v", c - 1)]))
                if G_N > 0:
                    uw.append((sG, marks[("ylast_g", c - 1)]))
            B.act("sc", (lambda dt, upre: lambda e: e.activation(
                u_s[:, dt * T:(dt + 1) * T], upre, AF.Silu,
                bias=cbc_s[:, dt:dt + 1]))(dt, upre),
                waits=uw, incs=[(sS, 1)], tag=f"usilu{c}_{dt}")
            rot_upre.consumed(ku, sS, sS.n)
        for dt in range(NDT):
            B.act("ve", (lambda dt: lambda e: e.tensor_copy(
                halo_s[:, dt * 3:dt * 3 + 3],
                ap3(xpart, dt * XPW + 256, [[NDT * XPW, 128], [1, 3]])))(dt),
                incs=[(sV, 1)], tag=f"halos{c}_{dt}")
        marks[("halosave", c)] = sV.n
        sc_u = sS.n
        if c == 0 and MID_DUMPS:
            for nm, t, wv in (("md1_xpart", a2, [(sS, sc_u)]),
                              ("md1_u", a5, [(sS, sc_u)])):
                dmp = nc.dram_tensor(nm, list(t.shape), t.dtype, kind="ExternalOutput")
                B.act("sy", (lambda dmp, t: lambda e: e.dma_start(
                    out=dmp[:], in_=t[:]))(dmp, t),
                    waits=wv, incs=[(sQO, 16)], tag=nm)
        ubw = [(sT, marks[("attn_mm", c - 1)])] if c > 0 else []
        B.act("sc", lambda e: e.copy(ubf[:], u_s[:]),
              waits=ubw, incs=[(sS, 1)], tag=f"ubf{c}")
        sc_ubf = sS.n
        _tr(5, c, nch)

        slotX, bwX = banks.acquire()
        for k in range(NDT):
            B.act("te", (lambda b_, k: lambda e: e.matmul(
                b_[:], wxp_s[:, k * 128:(k + 1) * 128], ubf[:, k * T:(k + 1) * T],
                start=(k == 0), stop=(k == NDT - 1)))(slotX["t"], k),
                waits=(bwX + [(sS, sc_ubf)]) if k == 0 else [], incs=[(sT, 1)],
                tag=f"xdblA{c}_{k}")
        te_xdbl = sT.n
        _tr(5.1, c, nch)
        dtlrw = [(sT, te_xdbl)]
        if c > 0:
            dtlrw.append((sT, marks[("dt_mm", c - 1)]))
        B.act("sc", (lambda b_: lambda e: e.copy(dtlr_s[:], b_[0:64, :]))(slotX["t"]),
              waits=dtlrw, incs=[(sS, 1)], tag=f"dtlrev{c}")
        sc_dtlr = sS.n
        _tr(5.2, c, nch)
        bcw = [(sT, marks[("bc_mm", c - 1)])] if c > 0 else []
        B.act("sc", (lambda b_: lambda e: e.copy(
            bcst_s[64:80, :], b_[64:80, :]))(slotX["t"]),
              waits=bcw, incs=[(sS, 1)], tag=f"bcev{c}")
        sc_bc = sS.n
        Banks.release(slotX, sS, sc_bc)
        _tr(5.3, c, nch)

        for dt in range(NDT):
            slot, bw = banks.acquire()
            B.act("te", (lambda b_, dt: lambda e: e.matmul(
                b_[:], wdt_s[:, dt * 128:(dt + 1) * 128], dtlr_s[:],
                start=True, stop=True))(slot["t"], dt),
                waits=bw + ([(sS, sc_dtlr)] if dt == 0 else []), incs=[(sT, 1)],
                tag=f"dtmm{c}_{dt}")
            tv = sT.n
            sgw = [(sT, tv)]
            if dt == 0:
                sgw.append((sT, marks[("ffn2_mm", c)]))
                if c > 0:
                    sgw.append((sV, marks[("scan_done", c - 1)]))
            B.act("sc", (lambda b_, dt: lambda e: e.activation(
                sig[:, dt * T:(dt + 1) * T], b_[:], AF.Sigmoid, scale=-1.0,
                bias=ndtb_s[:, dt:dt + 1]))(slot["t"], dt),
                waits=sgw, incs=[(sS, 1)], tag=f"sig{c}_{dt}")
            Banks.release(slot, sS, sS.n)
        marks[("dt_mm", c)] = sT.n
        _tr(5.7, c, nch)
        B.act("sc", lambda e: e.activation(dneg[:], sig[:], AF.Ln),
              incs=[(sS, 1)], tag=f"lnsp{c}")
        sc_dneg = sS.n
        _tr(6, c, nch)
        if c == 0 and MID_DUMPS:
            for nm, t in (("md2_dneg", a1), ("md2_sig", a3)):
                dmp = nc.dram_tensor(nm, list(t.shape), t.dtype, kind="ExternalOutput")
                B.act("sy", (lambda dmp, t: lambda e: e.dma_start(
                    out=dmp[:], in_=t[:]))(dmp, t),
                    waits=[(sS, sc_dneg)], incs=[(sQO, 16)], tag=nm)

        duw = [(sS, sc_dneg), (sS, sc_u)]
        if c > 0:
            duw.append((sQO, marks[("ostore", c - 1)]))
        B.act("ve", lambda e: e.scalar_tensor_tensor(
            du[:], dneg[:], -1.0, u_s[:], OP.mult, OP.mult),
            waits=duw, incs=[(sV, 1)], tag=f"du{c}")
        marks[("du", c)] = sV.n
        for dt in range(NDT):
            B.act("ve", (lambda dt: lambda e: e.tensor_scalar(
                yv[:, dt * T:(dt + 1) * T], u_s[:, dt * T:(dt + 1) * T],
                dcol_s[:, dt:dt + 1], None, OP.mult))(dt),
                waits=[(sV, marks[("halosave", c)])] if dt == 0 else [],
                incs=[(sV, 1)], tag=f"yinit{c}_{dt}")
        ve_yinit = sV.n
        _tr(7, c, nch)
        if c == 0 and MID_DUMPS:
            for nm, t in (("md3_yv", a2), ("md3_du", a7)):
                dmp = nc.dram_tensor(nm, list(t.shape), t.dtype, kind="ExternalOutput")
                B.act("sy", (lambda dmp, t: lambda e: e.dma_start(
                    out=dmp[:], in_=t[:]))(dmp, t),
                    waits=[(sV, ve_yinit)], incs=[(sQO, 16)], tag=nm)
        if G_N > 0:
            gw = [(sV, ve_oacc)]
            if c > 0:
                gw.append((sV, marks[("ysum", c - 1)]))
            B.act("gp", lambda e: e.memset(yg[:], 0.0),
                  waits=gw, incs=[(sG, 1)], tag=f"yg0{c}")

        for n in range(DST):
            on_gp = (n >= DST - G_N)
            sEB = sG if on_gp else sV
            eng = "gp" if on_gp else "ve"
            kb, bap, bwb = rot_brep.next()
            slotB, bwB = banks.acquire()
            B.act("te", (lambda b_, n: lambda e: e.matmul(
                b_[:], sel_s[64:80, n * 128:(n + 1) * 128], bcst_s[64:80, :],
                start=True, stop=True))(slotB["t"], n),
                waits=bwB + ([(sS, sc_bc)] if n == 0 else []), incs=[(sT, 1)],
                tag=f"bbc{c}_{n}")
            tvB = sT.n
            kc, cap, bwc = rot_crep.next()
            slotC, bwC = banks.acquire()
            B.act("te", (lambda b_, n: lambda e: e.matmul(
                b_[:], sel_s[64:80, (8 + n) * 128:(9 + n) * 128], bcst_s[64:80, :],
                start=True, stop=True))(slotC["t"], n),
                waits=bwC, incs=[(sT, 1)], tag=f"cbc{c}_{n}")
            tvC = sT.n
            B.act("sc", (lambda b_, bap: lambda e: e.copy(bap, b_[:]))(slotB["t"], bap),
                  waits=[(sT, tvB)] + bwb, incs=[(sS, 1)], tag=f"bev{c}_{n}")
            Banks.release(slotB, sS, sS.n)
            B.act("sc", (lambda b_, cap: lambda e: e.copy(cap, b_[:]))(slotC["t"], cap),
                  waits=[(sT, tvC)] + bwc, incs=[(sS, 1)], tag=f"cev{c}_{n}")
            sc_bc_ev = sS.n
            Banks.release(slotC, sS, sc_bc_ev)
            brep_rep = ap3(brep_s, bap.offset, [[2 * T, 128], [0, FG], [1, T]])
            crep_rep = ap3(crep_s, cap.offset, [[2 * T, 128], [0, FG], [1, T]])
            for g in range(NGR):
                ka, dAoff, bwa = rot_dA.next()
                for i in range(FG):
                    dt = g * FG + i
                    w = []
                    if n == 0 and g == 0 and i == 0:
                        w = [(sS, sc_dneg), (sT, marks[("stats_mm", c)])]
                    B.act("sc", (lambda dAoff, i, dt, n: lambda e: e.activation(
                        a8[:, dAoff + i * T:dAoff + (i + 1) * T],
                        dneg[:, dt * T:(dt + 1) * T],
                        AF.Exp, scale=nega_s[:, dt * 8 + n:dt * 8 + n + 1]))(dAoff, i, dt, n),
                        waits=w + (bwa if i == 0 else []), incs=[(sS, 1)],
                        tag=f"dA{c}_{n}_{g}_{i}")
                sc_dA = sS.n
                kd, dboff, bwd = rot_dbu[on_gp].next()
                B.act(eng, (lambda dboff, g, brep_rep: lambda e: e.tensor_tensor(
                    slot3(a3, dboff, 4096), slot3(a7, g * FG * T, 4096),
                    brep_rep, OP.mult))(dboff, g, brep_rep),
                    waits=[(sV, marks[("du", c)]), (sS, sc_bc_ev)] + bwd,
                    incs=[(sEB, 1)], tag=f"dbu{c}_{n}_{g}")
                eb_dbu = sEB.n
                kh, hoff, bwh = rot_hp.next()
                for i in range(FG):
                    dt = g * FG + i
                    w = [(sS, sc_dA), (sEB, eb_dbu)] if i == 0 else []
                    B.act("ve", (lambda hoff, dAoff, dboff, i, dt, n: lambda e:
                          e.tensor_tensor_scan(
                        hpool_s[:, hoff + i * T:hoff + (i + 1) * T],
                        a8[:, dAoff + i * T:dAoff + (i + 1) * T],
                        a3[:, dboff + i * T:dboff + (i + 1) * T],
                        hcar_s[:, dt * 8 + n:dt * 8 + n + 1], OP.mult, OP.add))(
                        hoff, dAoff, dboff, i, dt, n),
                        waits=w + (bwh if i == 0 else []), incs=[(sV, 1)],
                        tag=f"scan{c}_{n}_{g}_{i}")
                for i in range(FG):
                    B.act("ve", (lambda hoff, g, n, i: lambda e: e.tensor_copy(
                        hcar_s[:, (g * FG + i) * 8 + n:(g * FG + i) * 8 + n + 1],
                        hpool_s[:, hoff + i * T + T - 1:hoff + i * T + T]))(hoff, g, n, i),
                        incs=[(sV, 1)], tag=f"carry{c}_{n}_{g}_{i}")
                ve_scan = sV.n
                rot_dA.consumed(ka, sV, ve_scan)
                rot_dbu[on_gp].consumed(kd, sV, ve_scan)
                kp, poff, bwp = rot_p[on_gp].next()
                pw = [(sV, ve_scan)] + bwp
                if g == 0 and (n == 0 or (on_gp and n == DST - G_N)):
                    pw.append((sV, ve_yinit))
                B.act(eng, (lambda poff, hoff, crep_rep: lambda e: e.tensor_tensor(
                    slot3(a5, poff, 4096), slot3(hpool_s, hoff, 2 * FG * T),
                    crep_rep, OP.mult))(poff, hoff, crep_rep),
                    waits=pw, incs=[(sEB, 1)], tag=f"p{c}_{n}_{g}")
                eb_p = sEB.n
                rot_hp.consumed(kh, sEB, eb_p)
                yten = yg if on_gp else yv
                B.act(eng, (lambda yten, poff, g: lambda e: e.tensor_tensor(
                    yten[:, g * FG * T:(g + 1) * FG * T],
                    a5[:, poff:poff + FG * T],
                    yten[:, g * FG * T:(g + 1) * FG * T], OP.add))(yten, poff, g),
                    incs=[(sEB, 1)], tag=f"y{c}_{n}_{g}")
                rot_p[on_gp].consumed(kp, sEB, sEB.n)
                last_dbu, last_p = eb_dbu, eb_p
            rot_brep.consumed(kb, sEB, last_dbu)
            rot_crep.consumed(kc, sEB, last_p)
        marks[("scan_done", c)] = sV.n
        marks[("ylast_v", c)] = sV.n
        marks[("bc_mm", c)] = sT.n
        _tr(8, c, nch)
        if c == 0 and MID_DUMPS:
            for nm, t in (("md_hcar", hcar_s), ("md_hpool", hpool_s),
                          ("md_a8", a8), ("md_a3", a3), ("md_yv", a2)):
                shp = list(t.shape)
                dmp = nc.dram_tensor(nm, shp, t.dtype, kind="ExternalOutput")
                B.act("sy", (lambda dmp, t: lambda e: e.dma_start(
                    out=dmp[:], in_=t[:]))(dmp, t),
                    waits=[(sV, sV.n)], incs=[(sQO, 16)], tag=nm)
        if G_N > 0:
            marks[("ylast_g", c)] = sG.n

        gatew = [(sV, marks[("ylast_v", c)])]
        if G_N > 0:
            gatew.append((sG, marks[("ylast_g", c)]))
            B.act("ve", lambda e: e.tensor_tensor(du[:], yv[:, 0:4096], yg[:], OP.add),
                  waits=gatew, incs=[(sV, 1)], tag=f"ysum{c}")
            marks[("ysum", c)] = sV.n
            ysrc = du
            gatew = []
        else:
            ysrc = yv
        ggw = gatew + ([(sT, marks[("attn_mm", c - 1)])] if c > 0 else [])
        B.act("ve", (lambda ysrc: lambda e: e.tensor_tensor(
            ygbf[:], ysrc[:, 0:4096], z_s[:], OP.mult))(ysrc),
            waits=ggw, incs=[(sV, 1)], tag=f"gate{c}")
        marks[("gate", c)] = sV.n
        ve_gate = sV.n
        _tr(9, c, nch)

        for m in range(8):
            kk, wap, wq = slab_load(rot_wo, wout_st, m, 2048, f"wout{c}_{m}")
            slot, bw = banks.acquire()
            for k in range(NDT):
                B.act("te", (lambda b_, wap, k: lambda e: e.matmul(
                    b_[:], wap[:, k * 128:(k + 1) * 128], ygbf[:, k * T:(k + 1) * T],
                    start=(k == 0), stop=(k == NDT - 1)))(slot["t"], wap, k),
                    waits=(bw + [(sQW, wq)] +
                           ([(sV, ve_gate)] if m == 0 else [])) if k == 0 else [],
                    incs=[(sT, 1)], tag=f"attn{c}_{m}_{k}")
            rot_wo.consumed(kk, sT, sT.n)
            tv = sT.n
            B.act("ve", (lambda b_, m: lambda e: e.tensor_tensor(
                out_s[:, m * T:(m + 1) * T], oacc_s[:, m * T:(m + 1) * T],
                b_[:], OP.add))(slot["t"], m),
                waits=[(sT, tv)] + ([(sV, ve_oacc)] if m == 0 else []),
                incs=[(sV, 1)], tag=f"fout{c}_{m}")
            Banks.release(slot, sV, sV.n)
        marks[("attn_mm", c)] = sT.n
        marks[("attnout", c)] = sV.n
        if c == 0 and MID_DUMPS:
            for nm, t in (("md4_ygbf", a4), ("md4_out", a7), ("md4_oacc", oacc_s),
                          ("md4_z", z_s), ("md4_yv", a2)):
                dmp = nc.dram_tensor(nm, list(t.shape), t.dtype, kind="ExternalOutput")
                B.act("sy", (lambda dmp, t: lambda e: e.dma_start(
                    out=dmp[:], in_=t[:]))(dmp, t),
                    waits=[(sV, marks[("attnout", c)])], incs=[(sQO, 16)], tag=nm)

        B.act("sy", (lambda t0: lambda e: e.dma_start(
            out=ap3(outT, t0, [[L, 128], [128 * L, NDM], [1, T]]),
            in_=ap3(out_s, 0, [[4096, 128], [T, NDM], [1, T]])))(t0),
            waits=[(sV, marks[("attnout", c)])], incs=[(sQO, 16)], tag=f"ost{c}")
        marks[("ostore", c)] = sQO.n
      except _StopEmit:
        pass

    if dumps:
        named = dict(xn=xn_s, z=z_s, u=u_s, dneg=dneg, bcst=bcst_s, dtlr=dtlr_s, sig=sig,
                     yv=yv, ygbf=ygbf, oacc=oacc_s, outb=out_s, hcar=hcar_s,
                     xpart=xpart, h1=a1, du=du, brep=brep_s, crep=crep_s,
                     nega=nega_s, cbc=cbc_s, ndtb=ndtb_s, dcol=dcol_s,
                     b1c=b1c_s, b2c=b2c_s, rows=rows_s, ubf=ubf, cwc=cwc_s, a8raw=a8, hpool=hpool_s, halo=halo_s)
        allw = [(sV, sV.n), (sS, sS.n), (sT, sT.n)] + ([(sG, sG.n)] if sG.n else [])
        for nm in dumps:
            t = named[nm]
            shp = list(t.shape) if hasattr(t, "shape") else None
            dmp = nc.dram_tensor("dmp_" + nm, shp, t.dtype, kind="ExternalOutput")
            B.act("sy", (lambda dmp, t: lambda e: e.dma_start(out=dmp[:], in_=t[:]))(dmp, t),
                  waits=allw, incs=[(sQO, 16)], tag="dump_" + nm)
            allw = []
    for _ in range(PAD_TE):
        B.act("te", lambda e: e.nop(), tag="padnop")
    if PAD_MM:
        mw = PAD_MM_M
        if PAD_MM_FORCE_BANKS:
            slotD = banks.slots[PAD_MM_FORCE_BANKS[0]]
            bwD = slotD["free"]; slotD["free"] = []
            slotD2 = banks.slots[PAD_MM_FORCE_BANKS[1]]
            bwD2 = slotD2["free"]; slotD2["free"] = []
        else:
            slotD, bwD = banks.acquire()
            slotD2, bwD2 = banks.acquire()
        for i in range(32):
            if PAD_MM_GROUP:
                if i >= 16:
                    continue
                k = i % 16
                B.act("te", (lambda b_, k: lambda e: e.matmul(
                    b_[:], wxp_s[:, k * 128:(k + 1) * 128],
                    ubf[:, k * T:(k + 1) * T],
                    start=(k == 0), stop=(k == 15)))(slotD["t"], k),
                    waits=(bwD + [(sS, sS.n)]) if i == 0 else [],
                    incs=[(sT, 1)], tag=f"padmm{i}")
            else:
                B.act("te", (lambda b_, mw: lambda e: e.matmul(
                    b_[0:mw, :], id_s[:, 0:mw], z_s[:, 0:T],
                    start=True, stop=True))(slotD["t"], mw),
                    waits=(bwD + [(sV, sV.n)]) if i == 0 else [],
                    incs=[(sT, 1)], tag=f"padmm{i}")
    B.act("sy", lambda e: e.nop(), waits=[(sQO, sQO.n)], tag="drain")

    B.simulate()

    for s in B.sems.values():
        s.h = ctx.enter_context(nc.semaphore(s.name))
    block = ctx.enter_context(nc.Block())

    def runner(engname):
        def run(e):
            hi = {}
            for a in B.prog[engname]:
                for (s, v) in a.waits:
                    if hi.get(s.name, 0) < v:
                        e.wait_ge(s.h, v)
                        hi[s.name] = v
                inst = a.fn(e)
                for (s, d) in a.incs:
                    inst.then_inc(s.h, d)
        return run

    block.sync(runner("sy"))
    block.tensor(runner("te"))
    block.scalar(runner("sc"))
    block.vector(runner("ve"))
    block.gpsimd(runner("gp"))
    ctx.close()
    return nc


def kernel(L0_attn_hidden_states, L0_attn_in_proj_w, L0_attn_conv_w, L0_attn_conv_b,
           L0_attn_x_proj_w, L0_attn_dt_proj_w, L0_attn_dt_proj_b, L0_attn_A_log,
           L0_attn_D, L0_attn_out_proj_w, L0_attn_U_param, L0_attn_V_param,
           L0_attn_low_rank_scale, L0_ffn_w1, L0_ffn_b1, L0_ffn_w2, L0_ffn_b2):
    f = np.float32
    hs = np.asarray(L0_attn_hidden_states, f)
    wcat = np.concatenate([np.asarray(L0_attn_in_proj_w, f).T,
                           np.asarray(L0_ffn_w1, f)], axis=1)
    wcat_sl = np.ascontiguousarray(
        wcat.reshape(8, 128, 64, 128).transpose(2, 1, 0, 3)).reshape(64 * 128, 1024)
    w2_sl = np.ascontiguousarray(np.asarray(L0_ffn_w2, f))
    woutT = np.asarray(L0_attn_out_proj_w, f).T
    wout_sl = np.ascontiguousarray(
        woutT.reshape(16, 128, 8, 128).transpose(2, 1, 0, 3)).reshape(8 * 128, 2048)
    _wx = np.asarray(L0_attn_x_proj_w, f).T.reshape(16, 128, 80).transpose(1, 0, 2)
    wxp_sl = np.zeros((128, 16, 128), f)
    wxp_sl[:, :, 0:80] = _wx
    wxp_sl = np.ascontiguousarray(wxp_sl).reshape(128, 16 * 128)
    wdt_sl = np.ascontiguousarray(np.asarray(L0_attn_dt_proj_w, f).T)
    sel = np.zeros((16, 16 * 128), f)
    for j in range(16):
        sel[j, j * 128:(j + 1) * 128] = 1.0
    shared = {
        "wcat_f": wcat_sl,
        "w2_f": w2_sl,
        "wout_f": wout_sl,
        "wxp_f": wxp_sl,
        "wdt_f": wdt_sl,
        "alog_f": np.ascontiguousarray(np.asarray(L0_attn_A_log, f)),
        "ut_f": np.ascontiguousarray(np.asarray(L0_attn_U_param, f).T),
        "v_f": np.ascontiguousarray(np.asarray(L0_attn_V_param, f)),
        "lrs_f": np.asarray(L0_attn_low_rank_scale, f).reshape(1, 1),
        "convw_f": np.ascontiguousarray(np.asarray(L0_attn_conv_w, f)),
        "convb_f": np.asarray(L0_attn_conv_b, f).reshape(16, 128),
        "dtb_f": np.asarray(L0_attn_dt_proj_b, f).reshape(16, 128),
        "dvec_f": np.asarray(L0_attn_D, f).reshape(16, 128),
        "b1_f": np.asarray(L0_ffn_b1, f).reshape(32, 128),
        "b2_f": np.asarray(L0_ffn_b2, f).reshape(8, 128),
        "ident_f": np.eye(128, dtype=f),
        "ones1_f": np.ones((1, 128), f),
        "onegs_f": np.full((1, 128), -1.0 / np.sqrt(8.0), f),
        "c1024_f": np.full((128, 128), 1.0 / 1024.0, ml_dtypes.bfloat16),
        "sel_f": sel,
    }
    in_maps = []
    for c in range(8):
        m = dict(shared)
        m["xT"] = np.ascontiguousarray(hs[c].T)
        in_maps.append(m)
    nc = build_nc()
    res = run_bass_kernel_spmd(nc, in_maps, core_ids=list(range(8)))
    out = np.stack([np.ascontiguousarray(r["outT"].T) for r in res.results], axis=0)
    return out


if __name__ == "__main__":
    nc = build_nc()
    n = sum(len(v) for v in nc.m.__dict__.get("functions", {})) if False else 0
    print("build OK")


# version 06
# speedup vs baseline: 1.0106x; 1.0106x over previous
import numpy as np
import ml_dtypes
from contextlib import ExitStack

import concourse.bass as bass
import concourse.mybir as mybir
from concourse.bass_utils import run_bass_kernel_spmd

F32 = mybir.dt.float32
BF16 = mybir.dt.bfloat16
AF = mybir.ActivationFunctionType
OP = mybir.AluOpType

DM = 1024
DI = 2048
HF = 4096
DST = 8
L = 4096
T = 256
NCH = L // T
NDM = 8
NDT = 16
NCT = 64
FG = 4
NGR = NDT // FG
XPW = T + 3

G_N = 0
DBG_ZERO_HALO = False
MID_DUMPS = False
DBG_SERIAL_CHUNKS = False
TRUNC_STAGE = None
PAD_TE = 0
PAD_MM = False
PAD_MM_M = 128
PAD_MM_GROUP = False
PAD_MM_FORCE_BANKS = None


class _StopEmit(Exception):
    pass

ENGS = ("sy", "te", "sc", "ve", "gp")


class Sem:
    __slots__ = ("name", "n", "h")

    def __init__(self, name):
        self.name, self.n, self.h = name, 0, None


class Action:
    __slots__ = ("waits", "incs", "fn", "tag")

    def __init__(self, waits, incs, fn, tag):
        self.waits, self.incs, self.fn, self.tag = waits, incs, fn, tag


class Builder:
    def __init__(self):
        self.prog = {e: [] for e in ENGS}
        self.sems = {}

    def sem(self, name):
        s = Sem(name)
        self.sems[name] = s
        return s

    def act(self, eng, fn, waits=(), incs=(), tag=""):
        waits = [(s, v) for (s, v) in waits if v > 0]
        incs = list(incs)
        for (s, d) in incs:
            s.n += d
        self.prog[eng].append(Action(waits, incs, fn, tag))

    def simulate(self):
        cnt = {s: 0 for s in self.sems.values()}
        ptr = {e: 0 for e in ENGS}
        total = sum(len(v) for v in self.prog.values())
        done = 0
        while done < total:
            progress = False
            for e in ENGS:
                while ptr[e] < len(self.prog[e]):
                    a = self.prog[e][ptr[e]]
                    if all(cnt[s] >= v for (s, v) in a.waits):
                        for (s, d) in a.incs:
                            cnt[s] += d
                        ptr[e] += 1
                        done += 1
                        progress = True
                    else:
                        break
            if not progress:
                msg = []
                for e in ENGS:
                    if ptr[e] < len(self.prog[e]):
                        a = self.prog[e][ptr[e]]
                        blocked = [(s.name, v, cnt[s]) for (s, v) in a.waits if cnt[s] < v]
                        msg.append(f"{e}[{ptr[e]}] tag={a.tag} blocked on {blocked}")
                raise RuntimeError("DEADLOCK in schedule:\n" + "\n".join(msg))
        for e in ENGS:
            for a in self.prog[e]:
                for (s, v) in a.waits:
                    assert v <= s.n, f"wait {s.name}>={v} exceeds final {s.n} ({a.tag})"


class Banks:
    def __init__(self, tensors):
        self.slots = [{"t": t, "free": []} for t in tensors]
        self.i = 0

    def acquire(self):
        s = self.slots[self.i % 8]
        self.i += 1
        w = s["free"]
        s["free"] = []
        return s, w

    @staticmethod
    def release(slot, sem, val):
        slot["free"] = [(sem, val)]


class Rot:
    def __init__(self, aps):
        self.aps = aps
        self.free = [[] for _ in aps]
        self.i = 0

    def next(self):
        k = self.i % len(self.aps)
        self.i += 1
        w = self.free[k]
        self.free[k] = []
        return k, self.aps[k], w

    def consumed(self, k, sem, val):
        self.free[k].append((sem, val))


def ap3(t, off, dims):
    return bass.AP(t, off, dims)


def build_nc(nch=NCH, dumps=()):
    nc = bass.Bass("TRN2", target_bir_lowering=False)
    B = Builder()

    xT = nc.dram_tensor("xT", [DM, L], F32, kind="ExternalInput")
    outT = nc.dram_tensor("outT", [DM, L], F32, kind="ExternalOutput")
    wcat_f = nc.dram_tensor("wcat_f", [NCT * 128, 8 * 128], F32, kind="ExternalInput")
    w2_f = nc.dram_tensor("w2_f", [HF, DM], F32, kind="ExternalInput")
    wout_f = nc.dram_tensor("wout_f", [8 * 128, 16 * 128], F32, kind="ExternalInput")
    wxp_f = nc.dram_tensor("wxp_f", [128, 16 * 128], F32, kind="ExternalInput")
    wdt_f = nc.dram_tensor("wdt_f", [64, DI], F32, kind="ExternalInput")
    alog_f = nc.dram_tensor("alog_f", [DI, DST], F32, kind="ExternalInput")
    ut_f = nc.dram_tensor("ut_f", [8, DI], F32, kind="ExternalInput")
    v_f = nc.dram_tensor("v_f", [8, 8], F32, kind="ExternalInput")
    lrs_f = nc.dram_tensor("lrs_f", [1, 1], F32, kind="ExternalInput")
    convw_f = nc.dram_tensor("convw_f", [DI, 4], F32, kind="ExternalInput")
    convb_f = nc.dram_tensor("convb_f", [16, 128], F32, kind="ExternalInput")
    dtb_f = nc.dram_tensor("dtb_f", [16, 128], F32, kind="ExternalInput")
    dvec_f = nc.dram_tensor("dvec_f", [16, 128], F32, kind="ExternalInput")
    b1_f = nc.dram_tensor("b1_f", [32, 128], F32, kind="ExternalInput")
    b2_f = nc.dram_tensor("b2_f", [8, 128], F32, kind="ExternalInput")
    ident_f = nc.dram_tensor("ident_f", [128, 128], F32, kind="ExternalInput")
    ones1_f = nc.dram_tensor("ones1_f", [1, 128], F32, kind="ExternalInput")
    onegs_f = nc.dram_tensor("onegs_f", [1, 128], F32, kind="ExternalInput")
    c1024_f = nc.dram_tensor("c1024_f", [128, 128], BF16, kind="ExternalInput")
    sel_f = nc.dram_tensor("sel_f", [16, 16 * 128], F32, kind="ExternalInput")
    wcat_st = nc.dram_tensor("wcat_st", [NCT * 128, 8 * 128], BF16, kind="Internal")
    w2_st = nc.dram_tensor("w2_st", [HF, DM], BF16, kind="Internal")
    wout_st = nc.dram_tensor("wout_st", [8 * 128, 16 * 128], BF16, kind="Internal")

    ctx = ExitStack()
    sb = lambda n, s, d=F32: ctx.enter_context(nc.sbuf_tensor(n, s, d))
    ps = lambda n, s: ctx.enter_context(nc.psum_tensor(n, s, F32))

    a1 = sb("a1", [128, 4096])
    a2 = sb("a2", [128, NDT * XPW])
    a3 = sb("a3", [128, 4096])
    a4 = sb("a4", [128, 4096], BF16)
    a5 = sb("a5", [128, 4096])
    a6 = sb("a6", [128, 4096])
    a7 = sb("a7", [128, 4096])
    a8 = sb("a8", [128, 2048])
    xb_s = sb("xb_s", [128, 2048], BF16)
    x2b_s = sb("x2b_s", [128, 2048], BF16)
    xn_s = sb("xn_s", [128, 2048], BF16)
    z_s = sb("z_s", [128, 4096])
    oacc_s = sb("oacc_s", [128, 2048])
    hpool_s = sb("hpool_s", [128, 2 * FG * T])
    hcar_s = sb("hcar_s", [128, 128])
    brep_s = sb("brep_s", [128, 2 * T])
    crep_s = sb("crep_s", [128, 2 * T])
    bcst_s = sb("bcst_s", [128, T])
    lnt_s = sb("lnt_s", [128, 3 * T])
    tmpln_s = sb("tmpln_s", [128, 2 * T])
    cacc_s = sb("cacc_s", [128, 2 * T])
    upre_s = sb("upre_s", [128, 2 * T])
    dtlr_s = sb("dtlr_s", [64, T], BF16)
    halo_s = sb("halo_s", [128, 48])
    sel_s = sb("sel_s", [128, 16 * 128])
    id_s = sb("id_s", [128, 128])
    ones1_s = sb("ones1_s", [1, 128])
    onegs_s = sb("onegs_s", [1, 128])
    c1024_s = sb("c1024_s", [128, 128], BF16)
    alog_s = sb("alog_s", [128, 128])
    uv_s = sb("uv_s", [128, 128])
    ea_s = sb("ea_s", [128, 128])
    nega_s = sb("nega_s", [128, 128])
    snegc_s = sb("snegc_s", [128, 1])
    cwc_s = sb("cwc_s", [128, 64])
    cbc_s = sb("cbc_s", [128, 16])
    ndtb_s = sb("ndtb_s", [128, 16])
    dcol_s = sb("dcol_s", [128, 16])
    b1c_s = sb("b1c_s", [128, 32])
    b2c_s = sb("b2c_s", [128, 8])
    wxp_s = sb("wxp_s", [128, 16 * 128], BF16)
    wdt_s = sb("wdt_s", [64, DI], BF16)
    wcat_b = sb("wcat_b", [128, 2 * 1024], BF16)
    w2_b = sb("w2_b", [128, 2 * 1024], BF16)
    wout_b = sb("wout_b", [128, 2 * 2048], BF16)

    banks = Banks([ps(f"pb{i}", [128, T]) for i in range(8)])

    a1_bf = a1.ap().bitcast(BF16)
    a8_bf = a8.ap().bitcast(BF16)

    def h1_ap(k):
        return a1_bf[:, k * T:(k + 1) * T]

    dneg, xpart, yv, sig = a1, a2, a2, a3
    ubf = ygbf = a4
    u_s = a5
    x_s, yg = a6, a6
    du, out_s = a7, a7
    xbf_ap = xb_s[:, 0:2048]
    x2bf_ap = x2b_s[:, 0:2048]

    def slot3(arena, off, row):
        return ap3(arena, off, [[row, 128], [T, FG], [1, T]])

    dbu_off_v = [0, FG * T]
    dbu_off_g = [2 * FG * T, 3 * FG * T]
    p_off_v = [0, FG * T]
    p_off_g = [2 * FG * T, 3 * FG * T]
    dA_off = [0, FG * T]

    sQP = B.sem("sQP")
    sQW = B.sem("sQW")
    sQX = B.sem("sQX")
    sQO = B.sem("sQO")
    sT = B.sem("sT")
    sS = B.sem("sS")
    sV = B.sem("sV")
    sG = B.sem("sG")

    marks = {}

    pre = [
        (id_s[:], ident_f[:]), (ones1_s[:], ones1_f[:]), (onegs_s[:], onegs_f[:]),
        (c1024_s[:], c1024_f[:]), (sel_s[64:80, :], sel_f[:]),
        (wxp_s[:], wxp_f[:]), (wdt_s[:], wdt_f[:]),
        (a6[0:8, 0:2048], ut_f[:]),
        (a6[0:8, 2048:2056], v_f[:]),
        (a6[0:1, 2056:2057], lrs_f[:]),
        (a7[0:16, 0:128], convb_f[:]), (a7[0:16, 128:256], dtb_f[:]),
        (a7[0:16, 256:384], dvec_f[:]), (a7[0:32, 384:512], b1_f[:]),
        (a7[0:8, 512:640], b2_f[:]),
    ]
    for (dst, src) in pre:
        B.act("gp", (lambda d, s: lambda e: e.dma_start(out=d, in_=s))(dst, src),
              incs=[(sQP, 16)], tag="pre_dma")
    for dt in range(NDT):
        B.act("gp", (lambda dt: lambda e: e.dma_start(
            out=alog_s[:, dt * 8:(dt + 1) * 8],
            in_=alog_f[dt * 128:(dt + 1) * 128, :]))(dt),
            incs=[(sQP, 16)], tag="pre_alog")
        B.act("gp", (lambda dt: lambda e: e.dma_start(
            out=cwc_s[:, dt * 4:(dt + 1) * 4],
            in_=convw_f[dt * 128:(dt + 1) * 128, :]))(dt),
            incs=[(sQP, 16)], tag="pre_cw")
    n_const = sQP.n
    for i in range(4):
        B.act("gp", (lambda i: lambda e: e.dma_start(
            out=wcat_st[i * 2048:(i + 1) * 2048, :],
            in_=wcat_f[i * 2048:(i + 1) * 2048, :]))(i),
            incs=[(sQP, 16)], tag="stage_wcat")
    for i in range(2):
        B.act("gp", (lambda i: lambda e: e.dma_start(
            out=w2_st[i * 2048:(i + 1) * 2048, :],
            in_=w2_f[i * 2048:(i + 1) * 2048, :]))(i),
            incs=[(sQP, 16)], tag="stage_w2")
    B.act("gp", lambda e: e.dma_start(out=wout_st[:], in_=wout_f[:]),
          incs=[(sQP, 16)], tag="stage_wout")
    n_pro = sQP.n

    tr_jobs = [
        (a7[0:16, 0:128], 16, cbc_s[:, 0:16], False),
        (a7[0:16, 128:256], 16, ndtb_s[:, 0:16], True),
        (a7[0:16, 256:384], 16, dcol_s[:, 0:16], False),
        (a7[0:32, 384:512], 32, b1c_s[:, 0:32], False),
        (a7[0:8, 512:640], 8, b2c_s[:, 0:8], False),
    ]
    for idx, (src, r, dst, neg) in enumerate(tr_jobs):
        slot, bw = banks.acquire()
        bank = slot["t"]
        B.act("te", (lambda s_, r_, b_: lambda e: e.transpose(
            b_[:, 0:r_], s_, id_s[0:r_, 0:r_]))(src, r, bank),
            waits=bw + [(sQP, n_const)], incs=[(sT, 1)], tag=f"pre_tr{idx}")
        tv = sT.n
        B.act("sc", (lambda b_, r_, d_, n_: lambda e: e.activation(
            d_, b_[:, 0:r_], AF.Copy, scale=(-1.0 if n_ else 1.0)))(bank, r, dst, neg),
            waits=[(sT, tv)], incs=[(sS, 1)], tag=f"pre_trev{idx}")
        Banks.release(slot, sS, sS.n)
    for half in range(2):
        slot, bw = banks.acquire()
        for j in range(8):
            dt = half * 8 + j
            B.act("te", (lambda dt, b_, j: lambda e: e.matmul(
                b_[:, j * 8:(j + 1) * 8], a6[0:8, dt * 128:(dt + 1) * 128],
                a6[0:8, 2048:2056], start=True, stop=True))(dt, slot["t"], j),
                waits=(bw + [(sQP, n_const)]) if j == 0 else [],
                incs=[(sT, 1)], tag=f"pre_uv{dt}")
        tv = sT.n
        B.act("sc", (lambda b_, half: lambda e: e.copy(
            uv_s[:, half * 64:(half + 1) * 64], b_[:, 0:64]))(slot["t"], half),
            waits=[(sT, tv)], incs=[(sS, 1)], tag=f"pre_uvev{half}")
        Banks.release(slot, sS, sS.n)
    slot, bw = banks.acquire()
    B.act("te", (lambda b_: lambda e: e.matmul(
        b_[:, 0:1], onegs_s[:], a6[0:1, 2056:2057], start=True, stop=True))(slot["t"]),
        waits=bw + [(sQP, n_const)], incs=[(sT, 1)], tag="pre_sneg")
    B.act("sc", (lambda b_: lambda e: e.copy(snegc_s[:], b_[:, 0:1]))(slot["t"]),
          waits=[(sT, sT.n)], incs=[(sS, 1)], tag="pre_snegev")
    Banks.release(slot, sS, sS.n)
    B.act("sc", lambda e: e.activation(ea_s[:], alog_s[:], AF.Exp),
          waits=[(sQP, n_const)], incs=[(sS, 1)], tag="pre_ea")
    sc_pre = sS.n
    B.act("ve", lambda e: e.scalar_tensor_tensor(
        nega_s[:], uv_s[:], snegc_s[:], ea_s[:], OP.mult, OP.add),
        waits=[(sS, sc_pre)], incs=[(sV, 1)], tag="pre_negA")
    B.act("ve", lambda e: e.memset(hcar_s[:], 0.0), incs=[(sV, 1)], tag="pre_hc0")
    for dt in range(NDT):
        B.act("ve", (lambda dt: lambda e: e.memset(
            ap3(xpart, dt * XPW, [[NDT * XPW, 128], [1, 3]]), 0.0))(dt),
            incs=[(sV, 1)], tag=f"pre_halo0_{dt}")
    marks["pre_te"] = sT.n

    rot_cat = Rot([wcat_b[:, 0:1024], wcat_b[:, 1024:2048]])
    rot_w2 = Rot([w2_b[:, 0:1024], w2_b[:, 1024:2048]])
    rot_wo = Rot([wout_b[:, 0:2048], wout_b[:, 2048:4096]])
    rot_brep = Rot([brep_s[:, 0:T], brep_s[:, T:2 * T]])
    rot_crep = Rot([crep_s[:, 0:T], crep_s[:, T:2 * T]])
    rot_dA = Rot(dA_off)
    rot_hp = Rot([0, FG * T])
    rot_dbu = {False: Rot(dbu_off_v), True: Rot(dbu_off_g)}
    rot_p = {False: Rot(p_off_v), True: Rot(p_off_g)}
    rot_cacc = Rot([cacc_s[:, 0:T], cacc_s[:, T:2 * T]])
    rot_upre = Rot([upre_s[:, 0:T], upre_s[:, T:2 * T]])
    rot_tmp = Rot([tmpln_s[:, 0:T], tmpln_s[:, T:2 * T]])

    def _tr(k, c, nch):
        if c == nch - 1 and TRUNC_STAGE is not None and TRUNC_STAGE <= k:
            raise _StopEmit()

    for c in range(nch):
      try:
        t0 = c * T
        xw = [(sQP, n_pro), (sT, marks["pre_te"])] if c == 0 else \
             [(sV, marks[("xfree", c - 1)])]
        if c > 0 and DBG_SERIAL_CHUNKS:
            xw.append((sV, marks[("attnout", c - 1)]))
            xw.append((sQO, marks[("ostore", c - 1)]))
        if c > 0 and G_N > 0:
            xw.append((sV, marks[("ysum", c - 1)]))
        B.act("sy", (lambda t0: lambda e: e.dma_start(
            out=ap3(x_s, 0, [[4096, 128], [T, NDM], [1, T]]),
            in_=ap3(xT, t0, [[L, 128], [128 * L, NDM], [1, T]])))(t0),
            waits=xw, incs=[(sQX, 16)], tag=f"xload{c}")
        xld = sQX.n
        _tr(1, c, nch)

        def slab_load(rot, st_tensor, j, cols, tag):
            k, ap, bw = rot.next()
            w = list(bw)
            if c > 0 and DBG_SERIAL_CHUNKS:
                w.append((sV, marks[("attnout", c - 1)]))
            B.act("sy", (lambda ap, st_tensor, j, cols: lambda e: e.dma_start(
                out=ap, in_=st_tensor[j * 128:(j + 1) * 128, 0:cols]))(ap, st_tensor, j, cols),
                waits=w, incs=[(sQW, 16)], tag=tag)
            return k, ap, sQW.n

        xbw = [(sQX, xld)]
        B.act("sc", lambda e: e.copy(xbf_ap, x_s[:, 0:2048]),
              waits=xbw, incs=[(sS, 1)], tag=f"xbf{c}")
        B.act("sc", lambda e: e.activation(x2bf_ap, x_s[:, 0:2048], AF.Square),
              incs=[(sS, 1)], tag=f"x2bf{c}")
        sc_stats = sS.n
        slotM, bwM = banks.acquire()
        slotE, bwE = banks.acquire()
        for i in range(NDM):
            B.act("te", (lambda i, b_: lambda e: e.matmul(
                b_[:], c1024_s[:], xbf_ap[:, i * T:(i + 1) * T],
                start=(i == 0), stop=(i == NDM - 1)))(i, slotM["t"]),
                waits=(bwM + [(sS, sc_stats)]) if i == 0 else [], incs=[(sT, 1)],
                tag=f"mean{c}")
        for i in range(NDM):
            B.act("te", (lambda i, b_: lambda e: e.matmul(
                b_[:], c1024_s[:], x2bf_ap[:, i * T:(i + 1) * T],
                start=(i == 0), stop=(i == NDM - 1)))(i, slotE["t"]),
                waits=bwE if i == 0 else [], incs=[(sT, 1)], tag=f"ex2{c}")
        te_stats = sT.n
        marks[("stats_mm", c)] = te_stats
        lnt0 = lnt_s[:, 0:T]
        lnt1 = lnt_s[:, T:2 * T]
        lnt2 = lnt_s[:, 2 * T:3 * T]
        lw = [(sT, te_stats)]
        if c > 0:
            lw.append((sV, marks[("lnt_free", c - 1)]))
        B.act("sc", (lambda b_: lambda e: e.copy(lnt0, b_[:]))(slotM["t"]),
              waits=lw, incs=[(sS, 1)], tag=f"meanev{c}")
        B.act("sc", (lambda b_: lambda e: e.copy(lnt1, b_[:]))(slotE["t"]),
              incs=[(sS, 1)], tag=f"ex2ev{c}")
        sc_rows = sS.n
        Banks.release(slotM, sS, sc_rows)
        Banks.release(slotE, sS, sc_rows)
        B.act("ve", lambda e: e.tensor_tensor(lnt2, lnt0, lnt0, OP.mult),
              waits=[(sS, sc_rows)], incs=[(sV, 1)], tag=f"sq{c}")
        B.act("ve", lambda e: e.scalar_tensor_tensor(
            lnt2, lnt1, 1e-5, lnt2, OP.add, OP.subtract),
            incs=[(sV, 1)], tag=f"var{c}")
        ve_var = sV.n
        B.act("sc", lambda e: e.activation(lnt1, lnt2, AF.Sqrt),
              waits=[(sV, ve_var)], incs=[(sS, 1)], tag=f"sqrt{c}")
        sc_sqrt = sS.n
        B.act("ve", lambda e: e.reciprocal(lnt2, lnt1),
              waits=[(sS, sc_sqrt)], incs=[(sV, 1)], tag=f"recip{c}")
        B.act("ve", lambda e: e.tensor_tensor(lnt1, lnt0, lnt2, OP.mult),
              incs=[(sV, 1)], tag=f"mr{c}")
        ve_mr = sV.n
        marks[("rep_mm", c)] = sT.n
        xnw = []
        if c > 0:
            xnw.append((sT, marks[("cat_mm", c - 1)]))
        for i in range(NDM):
            kk, tmp, bwT = rot_tmp.next()
            B.act("ve", (lambda i, tmp: lambda e: e.tensor_tensor(
                tmp, x_s[:, i * T:(i + 1) * T], lnt2, OP.mult))(i, tmp),
                waits=(xnw if i == 0 else []) + bwT, incs=[(sV, 1)], tag=f"xr{c}_{i}")
            B.act("ve", (lambda i, tmp: lambda e: e.tensor_tensor(
                xn_s[:, i * T:(i + 1) * T], tmp, lnt1, OP.subtract))(i, tmp),
                incs=[(sV, 1)], tag=f"xn{c}_{i}")
            rot_tmp.consumed(kk, sV, sV.n)
        ve_xn = sV.n
        marks[("lnt_free", c)] = ve_xn
        _tr(2, c, nch)

        zxw = [(sV, marks[("gate", c - 1)])] if c > 0 else []
        h1w = [(sV, marks[("du", c - 1)])] if c > 0 else []
        for j in range(NCT):
            kk, wap, wq = slab_load(rot_cat, wcat_st, j, 1024, f"wcat{c}_{j}")
            slot, bw = banks.acquire()
            for k in range(8):
                B.act("te", (lambda b_, wap, k: lambda e: e.matmul(
                    b_[:], wap[:, k * 128:(k + 1) * 128], xn_s[:, k * T:(k + 1) * T],
                    start=(k == 0), stop=(k == 7)))(slot["t"], wap, k),
                    waits=(bw + [(sQW, wq)] +
                           ([(sV, ve_xn)] if j == 0 else [])) if k == 0 else [],
                    incs=[(sT, 1)], tag=f"cat{c}_{j}")
            rot_cat.consumed(kk, sT, sT.n)
            tv = sT.n
            if j < 16:
                B.act("sc", (lambda b_, j: lambda e: e.copy(
                    ap3(xpart, j * XPW + 3, [[NDT * XPW, 128], [1, T]]), b_[:]))(slot["t"], j),
                    waits=[(sT, tv)] + (zxw if j == 0 else []), incs=[(sS, 1)],
                    tag=f"xpev{c}_{j}")
            elif j < 32:
                B.act("sc", (lambda b_, j: lambda e: e.activation(
                    z_s[:, (j - 16) * T:(j - 15) * T], b_[:], AF.Silu))(slot["t"], j),
                    waits=[(sT, tv)], incs=[(sS, 1)], tag=f"zev{c}_{j}")
            else:
                B.act("sc", (lambda b_, j: lambda e: e.activation(
                    h1_ap(j - 32), b_[:], AF.Gelu,
                    bias=b1c_s[:, j - 32:j - 31]))(slot["t"], j),
                    waits=[(sT, tv)] + (h1w if j == 32 else []), incs=[(sS, 1)],
                    tag=f"h1ev{c}_{j}")
            Banks.release(slot, sS, sS.n)
            if j >= 32:
                marks[("h1", c, j - 32)] = sS.n
        marks[("cat_mm", c)] = sT.n
        sc_xpart_done = sS.n
        _tr(3, c, nch)
        if c == 0 and MID_DUMPS:
            for nm, t, wv in (("md0_xn", xn_s, [(sV, ve_xn)]),
                              ("md0_xpart", a2, [(sS, sc_xpart_done)]),
                              ("md0_z", z_s, [(sS, sc_xpart_done)]),
                              ("md0_h1", a1, [(sS, sc_xpart_done)]),
                              ("md0_x", a6, [(sQX, xld)])):
                dmp = nc.dram_tensor(nm, list(t.shape), t.dtype, kind="ExternalOutput")
                B.act("sy", (lambda dmp, t: lambda e: e.dma_start(
                    out=dmp[:], in_=t[:]))(dmp, t),
                    waits=wv, incs=[(sQO, 16)], tag=nm)

        fslots = []
        for m in range(8):
            slot, bw = banks.acquire()
            fslots.append((slot, bw))
        for k in range(32):
            kk, wap, wq = slab_load(rot_w2, w2_st, k, 1024, f"w2{c}_{k}")
            for m in range(8):
                slot, bw = fslots[m]
                w = []
                if m == 0:
                    w = [(sQW, wq), (sS, marks[("h1", c, k)])]
                if k == 0:
                    w = w + bw
                B.act("te", (lambda b_, wap, k, m: lambda e: e.matmul(
                    b_[:], wap[:, m * 128:(m + 1) * 128], h1_ap(k),
                    start=(k == 0), stop=(k == 31)))(slot["t"], wap, k, m),
                    waits=w, incs=[(sT, 1)], tag=f"ffn2{c}_{k}_{m}")
            rot_w2.consumed(kk, sT, sT.n)
        te_ffn2 = sT.n
        marks[("ffn2_mm", c)] = te_ffn2
        ow = [(sV, marks[("attnout", c - 1)])] if c > 0 else []
        for m in range(8):
            slot, _ = fslots[m]
            B.act("ve", (lambda b_, m: lambda e: e.scalar_tensor_tensor(
                oacc_s[:, m * T:(m + 1) * T], b_[:], b2c_s[:, m:m + 1],
                x_s[:, m * T:(m + 1) * T], OP.add, OP.add))(slot["t"], m),
                waits=([(sT, te_ffn2), (sQX, xld)] + ow) if m == 0 else [],
                incs=[(sV, 1)], tag=f"oacc{c}_{m}")
            Banks.release(slot, sV, sV.n)
        ve_oacc = sV.n
        marks[("xfree", c)] = ve_oacc
        _tr(4, c, nch)

        if c > 0:
            for dt in range(NDT):
                if DBG_ZERO_HALO:
                    B.act("ve", (lambda dt: lambda e: e.memset(
                        ap3(xpart, dt * XPW, [[NDT * XPW, 128], [1, 3]]), 0.0))(dt),
                        waits=[(sS, sc_xpart_done)] if dt == 0 else [],
                        incs=[(sV, 1)], tag=f"halor{c}_{dt}")
                else:
                    B.act("ve", (lambda dt: lambda e: e.tensor_copy(
                        ap3(xpart, dt * XPW, [[NDT * XPW, 128], [1, 3]]),
                        halo_s[:, dt * 3:dt * 3 + 3]))(dt),
                        waits=[(sS, sc_xpart_done)] if dt == 0 else [],
                        incs=[(sV, 1)], tag=f"halor{c}_{dt}")
        for dt in range(NDT):
            k0, acc0, bw0 = rot_cacc.next()
            B.act("ve", (lambda dt, acc0: lambda e: e.tensor_scalar(
                acc0, ap3(xpart, dt * XPW + 0, [[NDT * XPW, 128], [1, T]]),
                cwc_s[:, dt * 4:dt * 4 + 1], None, OP.mult))(dt, acc0),
                waits=([(sS, sc_xpart_done)] if dt == 0 else []) + bw0,
                incs=[(sV, 1)], tag=f"cv0_{c}_{dt}")
            k1, acc1, bw1 = rot_cacc.next()
            B.act("ve", (lambda dt, acc0, acc1: lambda e: e.scalar_tensor_tensor(
                acc1, ap3(xpart, dt * XPW + 1, [[NDT * XPW, 128], [1, T]]),
                cwc_s[:, dt * 4 + 1:dt * 4 + 2], acc0, OP.mult, OP.add))(dt, acc0, acc1),
                waits=bw1, incs=[(sV, 1)], tag=f"cv1_{c}_{dt}")
            rot_cacc.consumed(k0, sV, sV.n)
            k2, acc2, bw2 = rot_cacc.next()
            B.act("ve", (lambda dt, acc1, acc2: lambda e: e.scalar_tensor_tensor(
                acc2, ap3(xpart, dt * XPW + 2, [[NDT * XPW, 128], [1, T]]),
                cwc_s[:, dt * 4 + 2:dt * 4 + 3], acc1, OP.mult, OP.add))(dt, acc1, acc2),
                waits=bw2, incs=[(sV, 1)], tag=f"cv2_{c}_{dt}")
            rot_cacc.consumed(k1, sV, sV.n)
            ku, upre, bwu = rot_upre.next()
            B.act("ve", (lambda dt, acc2, upre: lambda e: e.scalar_tensor_tensor(
                upre, ap3(xpart, dt * XPW + 3, [[NDT * XPW, 128], [1, T]]),
                cwc_s[:, dt * 4 + 3:dt * 4 + 4], acc2, OP.mult, OP.add))(dt, acc2, upre),
                waits=bwu, incs=[(sV, 1)], tag=f"cv3_{c}_{dt}")
            rot_cacc.consumed(k2, sV, sV.n)
            ve_upre = sV.n
            uw = [(sV, ve_upre)]
            if dt == 0 and c > 0:
                uw.append((sV, marks[("ylast_v", c - 1)]))
                if G_N > 0:
                    uw.append((sG, marks[("ylast_g", c - 1)]))
            B.act("sc", (lambda dt, upre: lambda e: e.activation(
                u_s[:, dt * T:(dt + 1) * T], upre, AF.Silu,
                bias=cbc_s[:, dt:dt + 1]))(dt, upre),
                waits=uw, incs=[(sS, 1)], tag=f"usilu{c}_{dt}")
            rot_upre.consumed(ku, sS, sS.n)
        for dt in range(NDT):
            B.act("ve", (lambda dt: lambda e: e.tensor_copy(
                halo_s[:, dt * 3:dt * 3 + 3],
                ap3(xpart, dt * XPW + 256, [[NDT * XPW, 128], [1, 3]])))(dt),
                incs=[(sV, 1)], tag=f"halos{c}_{dt}")
        marks[("halosave", c)] = sV.n
        sc_u = sS.n
        if c == 0 and MID_DUMPS:
            for nm, t, wv in (("md1_xpart", a2, [(sS, sc_u)]),
                              ("md1_u", a5, [(sS, sc_u)])):
                dmp = nc.dram_tensor(nm, list(t.shape), t.dtype, kind="ExternalOutput")
                B.act("sy", (lambda dmp, t: lambda e: e.dma_start(
                    out=dmp[:], in_=t[:]))(dmp, t),
                    waits=wv, incs=[(sQO, 16)], tag=nm)
        ubw = [(sT, marks[("attn_mm", c - 1)])] if c > 0 else []
        B.act("sc", lambda e: e.copy(ubf[:], u_s[:]),
              waits=ubw, incs=[(sS, 1)], tag=f"ubf{c}")
        sc_ubf = sS.n
        _tr(5, c, nch)

        slotX, bwX = banks.acquire()
        for k in range(NDT):
            B.act("te", (lambda b_, k: lambda e: e.matmul(
                b_[:], wxp_s[:, k * 128:(k + 1) * 128], ubf[:, k * T:(k + 1) * T],
                start=(k == 0), stop=(k == NDT - 1)))(slotX["t"], k),
                waits=(bwX + [(sS, sc_ubf)]) if k == 0 else [], incs=[(sT, 1)],
                tag=f"xdblA{c}_{k}")
        te_xdbl = sT.n
        _tr(5.1, c, nch)
        dtlrw = [(sT, te_xdbl)]
        if c > 0:
            dtlrw.append((sT, marks[("dt_mm", c - 1)]))
        B.act("sc", (lambda b_: lambda e: e.copy(dtlr_s[:], b_[0:64, :]))(slotX["t"]),
              waits=dtlrw, incs=[(sS, 1)], tag=f"dtlrev{c}")
        sc_dtlr = sS.n
        _tr(5.2, c, nch)
        bcw = [(sT, marks[("bc_mm", c - 1)])] if c > 0 else []
        B.act("sc", (lambda b_: lambda e: e.copy(
            bcst_s[64:80, :], b_[64:80, :]))(slotX["t"]),
              waits=bcw, incs=[(sS, 1)], tag=f"bcev{c}")
        sc_bc = sS.n
        Banks.release(slotX, sS, sc_bc)
        _tr(5.3, c, nch)

        for dt in range(NDT):
            slot, bw = banks.acquire()
            B.act("te", (lambda b_, dt: lambda e: e.matmul(
                b_[:], wdt_s[:, dt * 128:(dt + 1) * 128], dtlr_s[:],
                start=True, stop=True))(slot["t"], dt),
                waits=bw + ([(sS, sc_dtlr)] if dt == 0 else []), incs=[(sT, 1)],
                tag=f"dtmm{c}_{dt}")
            tv = sT.n
            sgw = [(sT, tv)]
            if dt == 0:
                sgw.append((sT, marks[("ffn2_mm", c)]))
                if c > 0:
                    sgw.append((sV, marks[("scan_done", c - 1)]))
            B.act("sc", (lambda b_, dt: lambda e: e.activation(
                sig[:, dt * T:(dt + 1) * T], b_[:], AF.Sigmoid, scale=-1.0,
                bias=ndtb_s[:, dt:dt + 1]))(slot["t"], dt),
                waits=sgw, incs=[(sS, 1)], tag=f"sig{c}_{dt}")
            Banks.release(slot, sS, sS.n)
        marks[("dt_mm", c)] = sT.n
        _tr(5.7, c, nch)
        B.act("sc", lambda e: e.activation(dneg[:], sig[:], AF.Ln),
              incs=[(sS, 1)], tag=f"lnsp{c}")
        sc_dneg = sS.n
        _tr(6, c, nch)
        if c == 0 and MID_DUMPS:
            for nm, t in (("md2_dneg", a1), ("md2_sig", a3)):
                dmp = nc.dram_tensor(nm, list(t.shape), t.dtype, kind="ExternalOutput")
                B.act("sy", (lambda dmp, t: lambda e: e.dma_start(
                    out=dmp[:], in_=t[:]))(dmp, t),
                    waits=[(sS, sc_dneg)], incs=[(sQO, 16)], tag=nm)

        duw = [(sS, sc_dneg), (sS, sc_u)]
        if c > 0:
            duw.append((sQO, marks[("ostore", c - 1)]))
        B.act("ve", lambda e: e.scalar_tensor_tensor(
            du[:], dneg[:], -1.0, u_s[:], OP.mult, OP.mult),
            waits=duw, incs=[(sV, 1)], tag=f"du{c}")
        marks[("du", c)] = sV.n
        for dt in range(NDT):
            B.act("ve", (lambda dt: lambda e: e.tensor_scalar(
                yv[:, dt * T:(dt + 1) * T], u_s[:, dt * T:(dt + 1) * T],
                dcol_s[:, dt:dt + 1], None, OP.mult))(dt),
                waits=[(sV, marks[("halosave", c)])] if dt == 0 else [],
                incs=[(sV, 1)], tag=f"yinit{c}_{dt}")
        ve_yinit = sV.n
        _tr(7, c, nch)
        if c == 0 and MID_DUMPS:
            for nm, t in (("md3_yv", a2), ("md3_du", a7)):
                dmp = nc.dram_tensor(nm, list(t.shape), t.dtype, kind="ExternalOutput")
                B.act("sy", (lambda dmp, t: lambda e: e.dma_start(
                    out=dmp[:], in_=t[:]))(dmp, t),
                    waits=[(sV, ve_yinit)], incs=[(sQO, 16)], tag=nm)
        if G_N > 0:
            gw = [(sV, ve_oacc)]
            if c > 0:
                gw.append((sV, marks[("ysum", c - 1)]))
            B.act("gp", lambda e: e.memset(yg[:], 0.0),
                  waits=gw, incs=[(sG, 1)], tag=f"yg0{c}")

        for n in range(DST):
            on_gp = (n >= DST - G_N)
            sEB = sG if on_gp else sV
            eng = "gp" if on_gp else "ve"
            kb, bap, bwb = rot_brep.next()
            slotB, bwB = banks.acquire()
            B.act("te", (lambda b_, n: lambda e: e.matmul(
                b_[:], sel_s[64:80, n * 128:(n + 1) * 128], bcst_s[64:80, :],
                start=True, stop=True))(slotB["t"], n),
                waits=bwB + ([(sS, sc_bc)] if n == 0 else []), incs=[(sT, 1)],
                tag=f"bbc{c}_{n}")
            tvB = sT.n
            kc, cap, bwc = rot_crep.next()
            slotC, bwC = banks.acquire()
            B.act("te", (lambda b_, n: lambda e: e.matmul(
                b_[:], sel_s[64:80, (8 + n) * 128:(9 + n) * 128], bcst_s[64:80, :],
                start=True, stop=True))(slotC["t"], n),
                waits=bwC, incs=[(sT, 1)], tag=f"cbc{c}_{n}")
            tvC = sT.n
            B.act("sc", (lambda b_, bap: lambda e: e.copy(bap, b_[:]))(slotB["t"], bap),
                  waits=[(sT, tvB)] + bwb, incs=[(sS, 1)], tag=f"bev{c}_{n}")
            Banks.release(slotB, sS, sS.n)
            B.act("sc", (lambda b_, cap: lambda e: e.copy(cap, b_[:]))(slotC["t"], cap),
                  waits=[(sT, tvC)] + bwc, incs=[(sS, 1)], tag=f"cev{c}_{n}")
            sc_bc_ev = sS.n
            Banks.release(slotC, sS, sc_bc_ev)
            brep_rep = ap3(brep_s, bap.offset, [[2 * T, 128], [0, FG], [1, T]])
            crep_rep = ap3(crep_s, cap.offset, [[2 * T, 128], [0, FG], [1, T]])
            for g in range(NGR):
                ka, dAoff, bwa = rot_dA.next()
                for i in range(FG):
                    dt = g * FG + i
                    w = []
                    if n == 0 and g == 0 and i == 0:
                        w = [(sS, sc_dneg)]
                    B.act("sc", (lambda dAoff, i, dt, n: lambda e: e.activation(
                        a8[:, dAoff + i * T:dAoff + (i + 1) * T],
                        dneg[:, dt * T:(dt + 1) * T],
                        AF.Exp, scale=nega_s[:, dt * 8 + n:dt * 8 + n + 1]))(dAoff, i, dt, n),
                        waits=w + (bwa if i == 0 else []), incs=[(sS, 1)],
                        tag=f"dA{c}_{n}_{g}_{i}")
                sc_dA = sS.n
                kd, dboff, bwd = rot_dbu[on_gp].next()
                B.act(eng, (lambda dboff, g, brep_rep: lambda e: e.tensor_tensor(
                    slot3(a3, dboff, 4096), slot3(a7, g * FG * T, 4096),
                    brep_rep, OP.mult))(dboff, g, brep_rep),
                    waits=[(sV, marks[("du", c)]), (sS, sc_bc_ev)] + bwd,
                    incs=[(sEB, 1)], tag=f"dbu{c}_{n}_{g}")
                eb_dbu = sEB.n
                kh, hoff, bwh = rot_hp.next()
                for i in range(FG):
                    dt = g * FG + i
                    w = [(sS, sc_dA), (sEB, eb_dbu)] if i == 0 else []
                    B.act("ve", (lambda hoff, dAoff, dboff, i, dt, n: lambda e:
                          e.tensor_tensor_scan(
                        hpool_s[:, hoff + i * T:hoff + (i + 1) * T],
                        a8[:, dAoff + i * T:dAoff + (i + 1) * T],
                        a3[:, dboff + i * T:dboff + (i + 1) * T],
                        hcar_s[:, dt * 8 + n:dt * 8 + n + 1], OP.mult, OP.add))(
                        hoff, dAoff, dboff, i, dt, n),
                        waits=w + (bwh if i == 0 else []), incs=[(sV, 1)],
                        tag=f"scan{c}_{n}_{g}_{i}")
                for i in range(FG):
                    B.act("ve", (lambda hoff, g, n, i: lambda e: e.tensor_copy(
                        hcar_s[:, (g * FG + i) * 8 + n:(g * FG + i) * 8 + n + 1],
                        hpool_s[:, hoff + i * T + T - 1:hoff + i * T + T]))(hoff, g, n, i),
                        incs=[(sV, 1)], tag=f"carry{c}_{n}_{g}_{i}")
                ve_scan = sV.n
                rot_dA.consumed(ka, sV, ve_scan)
                rot_dbu[on_gp].consumed(kd, sV, ve_scan)
                kp, poff, bwp = rot_p[on_gp].next()
                pw = [(sV, ve_scan)] + bwp
                if g == 0 and (n == 0 or (on_gp and n == DST - G_N)):
                    pw.append((sV, ve_yinit))
                B.act(eng, (lambda poff, hoff, crep_rep: lambda e: e.tensor_tensor(
                    slot3(a5, poff, 4096), slot3(hpool_s, hoff, 2 * FG * T),
                    crep_rep, OP.mult))(poff, hoff, crep_rep),
                    waits=pw, incs=[(sEB, 1)], tag=f"p{c}_{n}_{g}")
                eb_p = sEB.n
                rot_hp.consumed(kh, sEB, eb_p)
                yten = yg if on_gp else yv
                B.act(eng, (lambda yten, poff, g: lambda e: e.tensor_tensor(
                    yten[:, g * FG * T:(g + 1) * FG * T],
                    a5[:, poff:poff + FG * T],
                    yten[:, g * FG * T:(g + 1) * FG * T], OP.add))(yten, poff, g),
                    incs=[(sEB, 1)], tag=f"y{c}_{n}_{g}")
                rot_p[on_gp].consumed(kp, sEB, sEB.n)
                last_dbu, last_p = eb_dbu, eb_p
            rot_brep.consumed(kb, sEB, last_dbu)
            rot_crep.consumed(kc, sEB, last_p)
        marks[("scan_done", c)] = sV.n
        marks[("ylast_v", c)] = sV.n
        marks[("bc_mm", c)] = sT.n
        _tr(8, c, nch)
        if c == 0 and MID_DUMPS:
            for nm, t in (("md_hcar", hcar_s), ("md_hpool", hpool_s),
                          ("md_a8", a8), ("md_a3", a3), ("md_yv", a2)):
                shp = list(t.shape)
                dmp = nc.dram_tensor(nm, shp, t.dtype, kind="ExternalOutput")
                B.act("sy", (lambda dmp, t: lambda e: e.dma_start(
                    out=dmp[:], in_=t[:]))(dmp, t),
                    waits=[(sV, sV.n)], incs=[(sQO, 16)], tag=nm)
        if G_N > 0:
            marks[("ylast_g", c)] = sG.n

        gatew = [(sV, marks[("ylast_v", c)])]
        if G_N > 0:
            gatew.append((sG, marks[("ylast_g", c)]))
            B.act("ve", lambda e: e.tensor_tensor(du[:], yv[:, 0:4096], yg[:], OP.add),
                  waits=gatew, incs=[(sV, 1)], tag=f"ysum{c}")
            marks[("ysum", c)] = sV.n
            ysrc = du
            gatew = []
        else:
            ysrc = yv
        ggw = gatew + ([(sT, marks[("attn_mm", c - 1)])] if c > 0 else [])
        B.act("ve", (lambda ysrc: lambda e: e.tensor_tensor(
            ygbf[:], ysrc[:, 0:4096], z_s[:], OP.mult))(ysrc),
            waits=ggw, incs=[(sV, 1)], tag=f"gate{c}")
        marks[("gate", c)] = sV.n
        ve_gate = sV.n
        _tr(9, c, nch)

        for m in range(8):
            kk, wap, wq = slab_load(rot_wo, wout_st, m, 2048, f"wout{c}_{m}")
            slot, bw = banks.acquire()
            for k in range(NDT):
                B.act("te", (lambda b_, wap, k: lambda e: e.matmul(
                    b_[:], wap[:, k * 128:(k + 1) * 128], ygbf[:, k * T:(k + 1) * T],
                    start=(k == 0), stop=(k == NDT - 1)))(slot["t"], wap, k),
                    waits=(bw + [(sQW, wq)] +
                           ([(sV, ve_gate)] if m == 0 else [])) if k == 0 else [],
                    incs=[(sT, 1)], tag=f"attn{c}_{m}_{k}")
            rot_wo.consumed(kk, sT, sT.n)
            tv = sT.n
            B.act("ve", (lambda b_, m: lambda e: e.tensor_tensor(
                out_s[:, m * T:(m + 1) * T], oacc_s[:, m * T:(m + 1) * T],
                b_[:], OP.add))(slot["t"], m),
                waits=[(sT, tv)] + ([(sV, ve_oacc)] if m == 0 else []),
                incs=[(sV, 1)], tag=f"fout{c}_{m}")
            Banks.release(slot, sV, sV.n)
        marks[("attn_mm", c)] = sT.n
        marks[("attnout", c)] = sV.n
        if c == 0 and MID_DUMPS:
            for nm, t in (("md4_ygbf", a4), ("md4_out", a7), ("md4_oacc", oacc_s),
                          ("md4_z", z_s), ("md4_yv", a2)):
                dmp = nc.dram_tensor(nm, list(t.shape), t.dtype, kind="ExternalOutput")
                B.act("sy", (lambda dmp, t: lambda e: e.dma_start(
                    out=dmp[:], in_=t[:]))(dmp, t),
                    waits=[(sV, marks[("attnout", c)])], incs=[(sQO, 16)], tag=nm)

        B.act("sy", (lambda t0: lambda e: e.dma_start(
            out=ap3(outT, t0, [[L, 128], [128 * L, NDM], [1, T]]),
            in_=ap3(out_s, 0, [[4096, 128], [T, NDM], [1, T]])))(t0),
            waits=[(sV, marks[("attnout", c)])], incs=[(sQO, 16)], tag=f"ost{c}")
        marks[("ostore", c)] = sQO.n
      except _StopEmit:
        pass

    if dumps:
        named = dict(xn=xn_s, z=z_s, u=u_s, dneg=dneg, bcst=bcst_s, dtlr=dtlr_s, sig=sig,
                     yv=yv, ygbf=ygbf, oacc=oacc_s, outb=out_s, hcar=hcar_s,
                     xpart=xpart, h1=a1, du=du, brep=brep_s, crep=crep_s,
                     nega=nega_s, cbc=cbc_s, ndtb=ndtb_s, dcol=dcol_s,
                     b1c=b1c_s, b2c=b2c_s, ubf=ubf, cwc=cwc_s, a8raw=a8, hpool=hpool_s, halo=halo_s)
        allw = [(sV, sV.n), (sS, sS.n), (sT, sT.n)] + ([(sG, sG.n)] if sG.n else [])
        for nm in dumps:
            t = named[nm]
            shp = list(t.shape) if hasattr(t, "shape") else None
            dmp = nc.dram_tensor("dmp_" + nm, shp, t.dtype, kind="ExternalOutput")
            B.act("sy", (lambda dmp, t: lambda e: e.dma_start(out=dmp[:], in_=t[:]))(dmp, t),
                  waits=allw, incs=[(sQO, 16)], tag="dump_" + nm)
            allw = []
    for _ in range(PAD_TE):
        B.act("te", lambda e: e.nop(), tag="padnop")
    if PAD_MM:
        mw = PAD_MM_M
        if PAD_MM_FORCE_BANKS:
            slotD = banks.slots[PAD_MM_FORCE_BANKS[0]]
            bwD = slotD["free"]; slotD["free"] = []
            slotD2 = banks.slots[PAD_MM_FORCE_BANKS[1]]
            bwD2 = slotD2["free"]; slotD2["free"] = []
        else:
            slotD, bwD = banks.acquire()
            slotD2, bwD2 = banks.acquire()
        for i in range(32):
            if PAD_MM_GROUP:
                if i >= 16:
                    continue
                k = i % 16
                B.act("te", (lambda b_, k: lambda e: e.matmul(
                    b_[:], wxp_s[:, k * 128:(k + 1) * 128],
                    ubf[:, k * T:(k + 1) * T],
                    start=(k == 0), stop=(k == 15)))(slotD["t"], k),
                    waits=(bwD + [(sS, sS.n)]) if i == 0 else [],
                    incs=[(sT, 1)], tag=f"padmm{i}")
            else:
                B.act("te", (lambda b_, mw: lambda e: e.matmul(
                    b_[0:mw, :], id_s[:, 0:mw], z_s[:, 0:T],
                    start=True, stop=True))(slotD["t"], mw),
                    waits=(bwD + [(sV, sV.n)]) if i == 0 else [],
                    incs=[(sT, 1)], tag=f"padmm{i}")
    B.act("sy", lambda e: e.nop(), waits=[(sQO, sQO.n)], tag="drain")

    B.simulate()

    for s in B.sems.values():
        s.h = ctx.enter_context(nc.semaphore(s.name))
    block = ctx.enter_context(nc.Block())

    def runner(engname):
        def run(e):
            hi = {}
            for a in B.prog[engname]:
                for (s, v) in a.waits:
                    if hi.get(s.name, 0) < v:
                        e.wait_ge(s.h, v)
                        hi[s.name] = v
                inst = a.fn(e)
                for (s, d) in a.incs:
                    inst.then_inc(s.h, d)
        return run

    block.sync(runner("sy"))
    block.tensor(runner("te"))
    block.scalar(runner("sc"))
    block.vector(runner("ve"))
    block.gpsimd(runner("gp"))
    ctx.close()
    return nc


def kernel(L0_attn_hidden_states, L0_attn_in_proj_w, L0_attn_conv_w, L0_attn_conv_b,
           L0_attn_x_proj_w, L0_attn_dt_proj_w, L0_attn_dt_proj_b, L0_attn_A_log,
           L0_attn_D, L0_attn_out_proj_w, L0_attn_U_param, L0_attn_V_param,
           L0_attn_low_rank_scale, L0_ffn_w1, L0_ffn_b1, L0_ffn_w2, L0_ffn_b2):
    f = np.float32
    hs = np.asarray(L0_attn_hidden_states, f)
    wcat = np.concatenate([np.asarray(L0_attn_in_proj_w, f).T,
                           np.asarray(L0_ffn_w1, f)], axis=1)
    wcat_sl = np.ascontiguousarray(
        wcat.reshape(8, 128, 64, 128).transpose(2, 1, 0, 3)).reshape(64 * 128, 1024)
    w2_sl = np.ascontiguousarray(np.asarray(L0_ffn_w2, f))
    woutT = np.asarray(L0_attn_out_proj_w, f).T
    wout_sl = np.ascontiguousarray(
        woutT.reshape(16, 128, 8, 128).transpose(2, 1, 0, 3)).reshape(8 * 128, 2048)
    _wx = np.asarray(L0_attn_x_proj_w, f).T.reshape(16, 128, 80).transpose(1, 0, 2)
    wxp_sl = np.zeros((128, 16, 128), f)
    wxp_sl[:, :, 0:80] = _wx
    wxp_sl = np.ascontiguousarray(wxp_sl).reshape(128, 16 * 128)
    wdt_sl = np.ascontiguousarray(np.asarray(L0_attn_dt_proj_w, f).T)
    sel = np.zeros((16, 16 * 128), f)
    for j in range(16):
        sel[j, j * 128:(j + 1) * 128] = 1.0
    shared = {
        "wcat_f": wcat_sl,
        "w2_f": w2_sl,
        "wout_f": wout_sl,
        "wxp_f": wxp_sl,
        "wdt_f": wdt_sl,
        "alog_f": np.ascontiguousarray(np.asarray(L0_attn_A_log, f)),
        "ut_f": np.ascontiguousarray(np.asarray(L0_attn_U_param, f).T),
        "v_f": np.ascontiguousarray(np.asarray(L0_attn_V_param, f)),
        "lrs_f": np.asarray(L0_attn_low_rank_scale, f).reshape(1, 1),
        "convw_f": np.ascontiguousarray(np.asarray(L0_attn_conv_w, f)),
        "convb_f": np.asarray(L0_attn_conv_b, f).reshape(16, 128),
        "dtb_f": np.asarray(L0_attn_dt_proj_b, f).reshape(16, 128),
        "dvec_f": np.asarray(L0_attn_D, f).reshape(16, 128),
        "b1_f": np.asarray(L0_ffn_b1, f).reshape(32, 128),
        "b2_f": np.asarray(L0_ffn_b2, f).reshape(8, 128),
        "ident_f": np.eye(128, dtype=f),
        "ones1_f": np.ones((1, 128), f),
        "onegs_f": np.full((1, 128), -1.0 / np.sqrt(8.0), f),
        "c1024_f": np.full((128, 128), 1.0 / 1024.0, ml_dtypes.bfloat16),
        "sel_f": sel,
    }
    in_maps = []
    for c in range(8):
        m = dict(shared)
        m["xT"] = np.ascontiguousarray(hs[c].T)
        in_maps.append(m)
    nc = build_nc()
    res = run_bass_kernel_spmd(nc, in_maps, core_ids=list(range(8)))
    out = np.stack([np.ascontiguousarray(r["outT"].T) for r in res.results], axis=0)
    return out


if __name__ == "__main__":
    nc = build_nc()
    n = sum(len(v) for v in nc.m.__dict__.get("functions", {})) if False else 0
    print("build OK")
